# version 20
# baseline (speedup 1.0000x reference)
import numpy as np
import concourse.bass as bass
import concourse.mybir as mybir
from concourse import library_config
from concourse.bass_utils import run_bass_kernel_spmd
from contextlib import ExitStack

F32 = mybir.dt.float32
BF16 = mybir.dt.bfloat16
AF = mybir.ActivationFunctionType
ALU = mybir.AluOpType
AX = mybir.AxisListType

NCORES = 8
B, S, D = 16, 2048, 1024
SEQ_PER_CORE = B // NCORES
TOK = SEQ_PER_CORE * S
TBS = S // 128
HP = 8
QBS = 4
KBS = 16
EPS_LN = 1e-5
EPS_LN4 = 1e-6

AR_N = 12288
A2_QT = 0
A2_KT = 1024
A2_WQ0 = 2048
A2_WQ1 = 2560
A2_WK0 = 3072
A2_WK1 = 3584
A2_WV = 4096
A2_PT0 = 4096
A2_PT1 = 4864
A2_SCR = 5632
A2_OTF = 6144
A2_SUMB = 8192
A2_RECB = 10240
A1_X0 = 0
A1_X1 = 1024
A1_HL = 2048
A1_HT = 3072
A1_R10 = 3584
A1_R11 = 4608
A1_Y = 5632
A1_XH = 6656
A1_R2 = 7680
A1_OUTF = 8704
A1_FT = 9728
A1_H2R = 10752
HLs = [A1_HL, A1_R10]
HTs = [A1_HT, A1_R11]
FTs = [A1_FT, A1_Y]
H2Rs = [A1_H2R, A1_XH]
PBs = [0, 3072]
GLs = [1024, 2688]


def build_nc():
    nc = bass.Bass("TRN2", target_bir_lowering=False)

    xd = nc.dram_tensor("x", [TOK, D], F32, kind="ExternalInput").ap()
    wqd = nc.dram_tensor("wq", [D, D], F32, kind="ExternalInput").ap()
    wkd = nc.dram_tensor("wk", [D, D], F32, kind="ExternalInput").ap()
    wvd = nc.dram_tensor("wv", [D, D], F32, kind="ExternalInput").ap()
    wpd = nc.dram_tensor("wp", [D, D], F32, kind="ExternalInput").ap()
    wlt = nc.dram_tensor("wlt", [D, D], F32, kind="ExternalInput").ap()
    gld = nc.dram_tensor("gld", [D, 40], F32, kind="ExternalInput").ap()
    rgt = nc.dram_tensor("rgt", [32, D], F32, kind="ExternalInput").ap()
    vec = nc.dram_tensor("vec", [4, D], F32, kind="ExternalInput").ap()
    outd = nc.dram_tensor("out", [TOK, D], F32, kind="ExternalOutput").ap()
    r1scr = nc.dram_tensor("r1scr", [TOK, D], F32, kind="Internal").ap()
    identd = nc.inline_tensor(np.eye(128, dtype=np.float32), "identd").ap()

    es = ExitStack()
    sb = lambda n, s, d: es.enter_context(nc.sbuf_tensor(n, s, d))

    wlin = sb("wlin", [128, 8 * 1024], BF16)
    wvp = sb("wvp", [128, 8 * 1024], BF16)
    glsb = sb("glsb", [128, 8 * 40], BF16)
    rgts = sb("rgts", [128, 1024], BF16)
    bprojb = sb("bprojb", [128, 1024], F32)
    halfb = sb("halfb", [128, 1024], F32)
    gammab = sb("gammab", [128, 1024], F32)
    betab = sb("betab", [128, 1024], F32)
    idt = sb("idt", [128, 128], F32)
    cpool = sb("cpool", [128, 8], F32)
    h2T = sb("h2T", [128, 8 * 2048], BF16)
    oT = sb("oT", [128, 8 * 2048], BF16)
    vhat = sb("vhat", [128, KBS * 1040], BF16)
    bstg = sb("bstg", [64, 512], F32)
    ln2m = sb("ln2m", [128, TBS], F32)
    ln2r = sb("ln2r", [128, TBS], F32)
    tTs = sb("tTs", [128, 128], BF16)
    sce_b = sb("sce", [128, 16], F32)
    top8_b = sb("top8", [128, 16], F32)
    tmp8_b = sb("tmp8", [128, 16], F32)
    tsc_b = sb("tsc", [128, 64], F32)
    ssum_b = sb("ssum", [128, 2], F32)
    rs8_b = sb("rs8", [128, 2], F32)
    hh3 = sb("hh3", [128, 2048], F32)
    bnst_b = sb("bnst", [128, 48], F32)
    mv2_b = sb("mv2", [128, 8], F32)
    rstd_b = sb("rstd", [128, 4], F32)

    def _sl(buf, i, n):
        return buf.ap()[:, n * i:n * (i + 1)]
    sceI = lambda p: _sl(sce_b, p, 8)
    top8I = lambda p: _sl(top8_b, p, 8)
    tmp8I = lambda p: _sl(tmp8_b, p, 8)
    tscI = lambda p: _sl(tsc_b, p, 32)
    ssumI = lambda p: _sl(ssum_b, p, 1)
    rs8I = lambda p: _sl(rs8_b, p, 1)
    bnstI = lambda i: _sl(bnst_b, i, 12)
    mv2I = lambda i: _sl(mv2_b, i, 2)
    rstdI = lambda i: _sl(rstd_b, i, 1)
    nbI = lambda i: _sl(bnst_b, i, 12)[:, 0:1]
    arena = sb("arena", [128, AR_N], F32)
    PS = es.enter_context(nc.psum_tensor("PS", [128, 4096], F32))

    def abf(off_f32, n_bf16, p0=0, p1=128):
        return arena.ap()[p0:p1, off_f32:off_f32 + n_bf16 // 2].bitcast(BF16)

    def af(off, n, p0=0, p1=128):
        return arena.ap()[p0:p1, off:off + n]

    semnames = ["sWc", "sWv", "sWq0", "sWq1", "sXa", "sXb", "sRa", "sRb",
                "sIN", "sSP", "sOU", "sBm", "sPE", "sDV", "sAC", "sGP"]
    SEM = {n: es.enter_context(nc.semaphore(n)) for n in semnames}
    CNT = {n: 0 for n in semnames}

    DRAIN_SET = set()

    def inc(name, instr, k=1):
        instr.then_inc(SEM[name], k)
        CNT[name] += k
        return CNT[name]

    def incd(name, engname, eng, instr, k=1):
        if engname in DRAIN_SET:
            return inc(name, eng.drain(), k)
        return inc(name, instr, k)

    def w(eng, name, tok=None):
        eng.wait_ge(SEM[name], CNT[name] if tok is None else tok)

    BLK = {}

    def on_raw(engname):
        def deco(f):
            getattr(BLK["b"], engname)(f)
        return deco

    def on(engname):
        def deco(f):
            if engname == "vector":
                def g(e):
                    e.wait_ge(SEM["sDV"], CNT["sDV"])
                    f(e)
            elif engname == "scalar":
                def g(e):
                    e.wait_ge(SEM["sAC"], CNT["sAC"])
                    f(e)
            else:
                g = f
            getattr(BLK["b"], engname)(g)
        return deco

    toks = {}
    LAST = {}

    with nc.Block() as block:
        BLK["b"] = block

        PRELOAD_ACT_TABLE = True

        @on("scalar")
        def _(s):
            if not PRELOAD_ACT_TABLE:
                return
            try:
                from concourse.hw_specs import get_activation_tables
                idx = None
                for i2, funcs in enumerate(
                        get_activation_tables(s.bass.m.arch).values()):
                    if AF.Ln in funcs and AF.Exp in funcs and AF.Identity in funcs:
                        idx = i2
                        break
                if idx is not None:
                    s.add_instruction(mybir.InstLoadActFuncSet(
                        name=s.bass.get_next_instruction_name(),
                        act_func_set_id=idx, ins=[], outs=[]))
            except Exception:
                pass

        @on("sync")
        def _(sy):
            inc("sIN", sy.dma_start(out=idt[:, :], in_=identd[:, :]), 16)
            for v in range(4):
                inc("sIN", sy.dma_start(out=arena.ap()[0:1, 1024 * v:1024 * v + 1024],
                                        in_=vec[v:v + 1, :]), 16)
        TOK_VEC = CNT["sIN"]

        @on("vector")
        def _(v):
            for i, val in enumerate([0.125, -20.0, EPS_LN, EPS_LN4, -0.5, 1.0]):
                v.memset(cpool[:, i:i + 1], val)
            v.memset(rgts.ap()[32:64, :], 0.0)
            v.memset(rgts.ap()[64:128, :], 0.0)
            v.memset(tTs.ap()[32:64, :], 0.0)
            v.memset(tTs.ap()[64:128, :], 0.0)
            incd("sDV", "vector", v, v.memset(
                bass.AP(vhat, 64, [[vhat.shape[1], 128], [1040, KBS], [65, 16], [1, 1]]), 1.0))
        TOK_MEMSET = CNT["sDV"]

        @on("gpsimd")
        def _(g):
            g.load_library(library_config.attn)
            for c in range(8):
                inc("sWc", g.dma_start(out=wlin.ap()[:, 1024 * c:1024 * (c + 1)],
                                       in_=wlt[128 * c:128 * (c + 1), :]), 16)
                inc("sWc", g.dma_start(out=glsb.ap()[:, 40 * c:40 * (c + 1)],
                                       in_=gld[128 * c:128 * (c + 1), :]), 16)
            inc("sWv", g.dma_start(
                out=wvp.ap(),
                in_=bass.AP(wvd.tensor, 0, [[1024, 128], [131072, 8], [1, 1024]])), 16)
            w(g, "sDV", TOK_MEMSET)
            inc("sWc", g.dma_start(out=rgts.ap()[0:32, :], in_=rgt[:, :]), 16)
            w(g, "sIN", TOK_VEC)
            inc("sGP", g.partition_broadcast(bprojb[:, :], arena.ap()[0:1, 0:1024], channels=128))
            inc("sGP", g.partition_broadcast(halfb[:, :], arena.ap()[0:1, 1024:2048], channels=128))
            inc("sGP", g.partition_broadcast(gammab[:, :], arena.ap()[0:1, 2048:3072], channels=128))
            inc("sGP", g.partition_broadcast(betab[:, :], arena.ap()[0:1, 3072:4096], channels=128))
        TOK_WALL = CNT["sWc"]
        TOK_BCV = CNT["sGP"]
        toks[("wvload", 0)] = CNT["sWv"]

        @on("vector")
        def _(v):
            w(v, "sGP", TOK_BCV)
            incd("sDV", "vector", v, v.tensor_scalar_mul(halfb[:, :], halfb[:, :], 0.5))
        TOK_CONSTS = CNT["sDV"]

        LNROT = [0]

        def emit_ln_stats(src_ap, eps_col, pre=(), gap=None):
            sbi = LNROT[0] % 4
            LNROT[0] += 1
            bnst, mv2, rstd = bnstI(sbi), mv2I(sbi), rstdI(sbi)

            @on("vector")
            def _(v):
                for nm, tk in pre:
                    w(v, nm, tk)
                v.bn_stats(bnst[:, 0:6], src_ap[:, 0:512])
                last = v.bn_stats(bnst[:, 6:12], src_ap[:, 512:1024])
                if gap is not None:
                    last = gap(v)
                inc("sDV", last)
                w(v, "sDV")
                inc("sDV", v.bn_aggr(mv2[:, :], bnst[:, 0:12]))
            t_stats = CNT["sDV"]

            @on("scalar")
            def _(s):
                w(s, "sDV", t_stats)
                inc("sAC", s.activation(out=rstd[:, :], in_=mv2[:, 1:2], func=AF.Ln,
                                        bias=cpool[:, eps_col:eps_col + 1]))
                w(s, "sAC")
                inc("sAC", s.activation(out=rstd[:, :], in_=rstd[:, :], func=AF.Exp,
                                        scale=cpool[:, 4:5]))
            return sbi, CNT["sAC"]

        def emit_ln_apply(src_ap, dst_ap, sbi, t_rstd, apply_pre=(),
                          store_tb=None):
            mv2, rstd = mv2I(sbi), rstdI(sbi)

            @on("vector")
            def _(v):
                w(v, "sAC", t_rstd)
                for nm, tk in apply_pre:
                    w(v, nm, tk)
                last = v.tensor_scalar(out=dst_ap, in0=src_ap, scalar1=mv2[:, 0:1],
                                       scalar2=rstd[:, :], op0=ALU.subtract, op1=ALU.mult)
                if store_tb is not None:
                    v.tensor_copy(ln2m[:, store_tb:store_tb + 1], mv2[:, 0:1])
                    last = v.tensor_copy(ln2r[:, store_tb:store_tb + 1], rstd[:, :])
                inc("sDV", last)
            return CNT["sDV"]

        def emit_ln(src_ap, dst_ap, eps_col, pre=(), store_tb=None, gap=None,
                    apply_pre=()):
            sbi, t_rstd = emit_ln_stats(src_ap, eps_col, pre=pre, gap=gap)
            return emit_ln_apply(src_ap, dst_ap, sbi, t_rstd,
                                 apply_pre=apply_pre, store_tb=store_tb)

        def emit_tr_pe(src_cols_ap, tok_src_dv, pe_pre=()):
            @on("tensor")
            def _(t):
                w(t, "sDV", tok_src_dv)
                for nm, tk in pe_pre:
                    w(t, nm, tk)
                for c in range(7):
                    t.transpose(PS[:, 1536 + 128 * c:1536 + 128 * (c + 1)],
                                src_cols_ap[:, 128 * c:128 * (c + 1)], idt[:, :])
                incd("sPE", "tensor", t, t.transpose(PS[:, 2432:2560],
                                       src_cols_ap[:, 896:1024], idt[:, :]))
            return CNT["sPE"]

        def emit_tr_evac(dst_bf16_ap, t_tr, evac_pre=(), eng="vector"):
            if eng == "vector":
                @on("vector")
                def _(v):
                    w(v, "sPE", t_tr)
                    for nm, tk in evac_pre:
                        w(v, nm, tk)
                    inc("sDV", v.tensor_copy(dst_bf16_ap, PS[:, 1536:2560]))
                return ("sDV", CNT["sDV"])

            @on("scalar")
            def _(s):
                w(s, "sPE", t_tr)
                for nm, tk in evac_pre:
                    w(s, nm, tk)
                inc("sAC", s.activation(out=dst_bf16_ap, in_=PS[:, 1536:2560],
                                        func=AF.Identity))
            return ("sAC", CNT["sAC"])

        def emit_transpose_block(src_cols_ap, dst_bf16_ap, tok_src_dv,
                                 pe_pre=(), evac_pre=(), eng="vector"):
            t_tr = emit_tr_pe(src_cols_ap, tok_src_dv, pe_pre)
            t_ev = emit_tr_evac(dst_bf16_ap, t_tr, evac_pre, eng=eng)
            return t_ev, t_tr

        def emit_ffn_mm(hT_ap, tok_hT_dv, par=0, pe_pre=()):
            pb, gl = PBs[par], GLs[par]

            @on("tensor")
            def _(t):
                w(t, tok_hT_dv[0], tok_hT_dv[1])
                w(t, "sWc", TOK_WALL)
                for nm, tk in pe_pre:
                    w(t, nm, tk)
                for c in range(8):
                    st = (c == 0)
                    t.matmul(PS[:, pb:pb + 512], hT_ap[:, 128 * c:128 * (c + 1)],
                             wlin.ap()[:, 1024 * c:1024 * c + 512], start=st, stop=False)
                    t.matmul(PS[:, pb + 512:pb + 1024], hT_ap[:, 128 * c:128 * (c + 1)],
                             wlin.ap()[:, 1024 * c + 512:1024 * (c + 1)], start=st, stop=False)
                    mm = t.matmul(PS[:, gl:gl + 40], hT_ap[:, 128 * c:128 * (c + 1)],
                                  glsb.ap()[:, 40 * c:40 * (c + 1)], start=st, stop=(c == 7))
                incd("sPE", "tensor", t, mm)
            return CNT["sPE"]

        def emit_ffn_gate(t_gl, par=0):
            pb, gl = PBs[par], GLs[par]
            sce, top8 = sceI(par), top8I(par)
            tmp8, tsc = tmp8I(par), tscI(par)
            ssum, rs8 = ssumI(par), rs8I(par)

            @on_raw("scalar")
            def _(s):
                w(s, "sPE", t_gl)
                incd("sAC", "scalar", s, s.activation(
                    out=sce[:, :], in_=PS[:, gl:gl + 8], func=AF.Exp))
            t_e8 = CNT["sAC"]

            @on("vector")
            def _(v):
                w(v, "sPE", t_gl)
                w(v, "sAC", t_e8)
                v.reduce_sum(ssum[:, :], sce[:, :], axis=AX.X)
                inc("sDV", v.max(out=top8[:, :], in_=sce[:, :]))
                w(v, "sDV")
                v.reciprocal(rs8[:, :], ssum[:, :])
                inc("sDV", v.memset(top8[:, 2:8], 0.0))
                w(v, "sDV")
                inc("sDV", v.match_replace(out=tmp8[:, :], in_to_replace=top8[:, :],
                                           in_values=sce[:, :], imm_value=0.0))
                w(v, "sDV")
                inc("sDV", v.tensor_sub(tmp8[:, :], sce[:, :], tmp8[:, :]))
                w(v, "sDV")
                inc("sDV", v.scalar_tensor_tensor(
                    out=tsc[:, :].rearrange("p (e r) -> p e r", r=4),
                    in0=PS[:, gl + 8:gl + 40].rearrange("p (e r) -> p e r", r=4),
                    scalar=rs8[:, :],
                    in1=tmp8[:, :, None].broadcast_to([128, 8, 4]),
                    op0=ALU.mult, op1=ALU.mult))
            return CNT["sDV"]

        def emit_ffn_tail2(out_ap, ft_ap, t_tsc, par=0, last_pre=()):
            pb, gl = PBs[par], GLs[par]
            tsc = tscI(par)

            @on("tensor")
            def _(t):
                w(t, "sDV", t_tsc)
                tke = LAST.get("tTs_evac_dv")
                if tke is not None:
                    w(t, "sDV", tke)
                incd("sPE", "tensor", t, t.transpose(PS[0:32, 2560:2688], tsc[:, :], idt[:, :]))
            t_tT = CNT["sPE"]

            @on("vector")
            def _(v):
                w(v, "sPE", t_tT)
                incd("sDV", "vector", v, v.tensor_copy(tTs.ap()[0:32, :], PS[0:32, 2560:2688]))
            t_tTe = CNT["sDV"]
            LAST["tTs_evac_dv"] = t_tTe

            @on("tensor")
            def _(t):
                w(t, "sDV", t_tTe)
                t.matmul(PS[:, pb:pb + 512], tTs.ap()[:, :], rgts.ap()[:, 0:512],
                         start=False, stop=True)
                incd("sPE", "tensor", t, t.matmul(PS[:, pb + 512:pb + 1024], tTs.ap()[:, :],
                                    rgts.ap()[:, 512:1024], start=False, stop=True))
            t_base = CNT["sPE"]

            return emit_ffn_last(out_ap, ft_ap, t_base, par=par,
                                 last_pre=last_pre)

        def emit_ffn_lora(t_tsc, par=0):
            pb, gl = PBs[par], GLs[par]
            tsc = tscI(par)

            @on("tensor")
            def _(t):
                w(t, "sDV", t_tsc)
                tke = LAST.get("tTs_evac_dv")
                if tke is not None:
                    w(t, "sDV", tke)
                incd("sPE", "tensor", t, t.transpose(PS[0:32, 2560:2688], tsc[:, :], idt[:, :]))
            t_tT = CNT["sPE"]

            @on("vector")
            def _(v):
                w(v, "sPE", t_tT)
                incd("sDV", "vector", v, v.tensor_copy(tTs.ap()[0:32, :], PS[0:32, 2560:2688]))
            t_tTe = CNT["sDV"]
            LAST["tTs_evac_dv"] = t_tTe

            @on("tensor")
            def _(t):
                w(t, "sDV", t_tTe)
                t.matmul(PS[:, pb:pb + 512], tTs.ap()[:, :], rgts.ap()[:, 0:512],
                         start=False, stop=True)
                incd("sPE", "tensor", t, t.matmul(PS[:, pb + 512:pb + 1024], tTs.ap()[:, :],
                                    rgts.ap()[:, 512:1024], start=False, stop=True))
            return CNT["sPE"]

        def emit_ffn_last(out_ap, ft_ap, t_base, par=0, last_pre=()):
            pb = PBs[par]

            @on("vector")
            def _(v):
                w(v, "sPE", t_base)
                for nm, tk in last_pre:
                    w(v, nm, tk)
                inc("sDV", v.scalar_tensor_tensor(out=out_ap, in0=PS[:, pb:pb + 1024],
                                                  scalar=0.5, in1=ft_ap,
                                                  op0=ALU.mult, op1=ALU.add))
            return CNT["sDV"]

        def p1_xload(seq, tb):
            p = tb % 2
            gtb = seq * TBS + tb
            xslot = A1_X0 if p == 0 else A1_X1
            sX = "sXa" if p == 0 else "sXb"

            @on("sync")
            def _(sy):
                tkf = toks.get(("xfree", seq, tb - 2))
                if tkf is not None:
                    w(sy, "sDV", tkf)
                if seq == 0 and tb == 0:
                    w(sy, "sDV", TOK_CONSTS)
                if seq == 1 and tb == 0:
                    w(sy, "sDV", toks[("p3done", 0)])
                inc(sX, sy.dma_start(out=af(xslot, 1024),
                                     in_=xd[128 * gtb:128 * (gtb + 1), :]), 16)
            toks[("xload", seq, tb)] = CNT[sX]

        def p1_ln1_stats(seq, tb):
            p = tb % 2
            xslot = A1_X0 if p == 0 else A1_X1
            sX = "sXa" if p == 0 else "sXb"
            pre = [(sX, toks[("xload", seq, tb)])]
            if seq == 0 and tb == 0:
                pre.append(("sDV", TOK_CONSTS))
            toks[("ln1i", seq, tb)] = emit_ln_stats(af(xslot, 1024), 2, pre=pre)

        def p1_ft(seq, tb):
            p = tb % 2
            xslot = A1_X0 if p == 0 else A1_X1
            sX = "sXa" if p == 0 else "sXb"

            @on("vector")
            def _(v):
                w(v, sX, toks[("xload", seq, tb)])
                inc("sDV", v.tensor_add(af(FTs[p], 1024), halfb[:, :],
                                        af(xslot, 1024)))

        def p1_ln1_apply(seq, tb):
            p = tb % 2
            xslot = A1_X0 if p == 0 else A1_X1
            apply_pre = []
            tk = toks.get(("trT", seq, tb - 2))
            if tk is not None:
                apply_pre.append(("sPE", tk))
            sbi, t_rstd = toks[("ln1i", seq, tb)]
            t_h1 = emit_ln_apply(af(xslot, 1024), af(HLs[p], 1024), sbi, t_rstd,
                                 apply_pre=apply_pre)
            toks[("xfree", seq, tb)] = t_h1
            toks[("h1", seq, tb)] = t_h1

        def p1_iter(seq, k):
            if k + 1 < TBS:
                p1_xload(seq, k + 1)

            if k < TBS:
                p = k % 2
                pe_pre = []
                if LAST.get("tr34_evac_dv") is not None:
                    pe_pre.append(LAST["tr34_evac_dv"])
                t_trT = emit_tr_pe(af(HLs[p], 1024), toks[("h1", seq, k)],
                                   pe_pre=pe_pre)
                toks[("trT", seq, k)] = t_trT

            if k >= 1:
                t_tsc = emit_ffn_gate(toks[("ffngl", seq, k - 1)],
                                      par=(k - 1) % 2)

            if k < TBS:
                p = k % 2
                evac_pre = []
                tk = toks.get(("ffngl", seq, k - 2))
                if tk is not None:
                    evac_pre.append(("sPE", tk))
                t_hT = emit_tr_evac(abf(HTs[p], 1024), t_trT, evac_pre=evac_pre,
                                    eng="scalar")
                LAST["tr34_evac_dv"] = t_hT
                toks[("hT", seq, k)] = t_hT

            if k >= 1:
                t_base = emit_ffn_lora(t_tsc, par=(k - 1) % 2)

            if k + 1 < TBS:
                p1_ln1_stats(seq, k + 1)

            if k >= 1:
                last_pre = []
                if LAST.get("outf_read") is not None:
                    last_pre.append(LAST["outf_read"])
                t_r1 = emit_ffn_last(af(A1_OUTF, 1024),
                                     af(FTs[(k - 1) % 2], 1024), t_base,
                                     par=(k - 1) % 2, last_pre=last_pre)
                toks[("ffnout", seq, k - 1)] = t_r1

                tb = k - 1
                gtb = seq * TBS + tb

                @on("sync")
                def _(sy):
                    w(sy, "sDV", toks[("ffnout", seq, tb)])
                    inc("sSP", sy.dma_start(
                        out=r1scr[128 * gtb:128 * (gtb + 1), :],
                        in_=af(A1_OUTF, 1024)), 16)
                LAST["outf_read"] = ("sSP", CNT["sSP"])

            if k + 1 < TBS:
                p1_ft(seq, k + 1)

            if k < TBS:
                p = k % 2
                ffn_pe_pre = []
                tk = toks.get(("ffnout", seq, k - 2))
                if tk is not None:
                    ffn_pe_pre.append(("sDV", tk))
                t_gl = emit_ffn_mm(abf(HTs[p], 1024), toks[("hT", seq, k)],
                                   par=p, pe_pre=ffn_pe_pre)
                toks[("ffngl", seq, k)] = t_gl

            if k >= 1:
                tb = k - 1
                toks[("ln2i", seq, tb)] = emit_ln_stats(af(A1_OUTF, 1024), 2)

            if k + 1 < TBS:
                p1_ln1_apply(seq, k + 1)

            if k >= 1:
                tb = k - 1
                p = tb % 2
                apply_pre2 = []
                tk = toks.get(("trH", seq, tb - 2))
                if tk is not None:
                    apply_pre2.append(("sPE", tk))
                sbi2, t_rstd2 = toks[("ln2i", seq, tb)]
                t_h2 = emit_ln_apply(af(A1_OUTF, 1024), af(H2Rs[p], 1024),
                                     sbi2, t_rstd2, apply_pre=apply_pre2,
                                     store_tb=tb)

                h2T_dst = bass.AP(h2T, 128 * tb,
                                  [[h2T.shape[1], 128], [2048, 8], [1, 128]])
                t_h2T, t_trh2 = emit_transpose_block(
                    af(H2Rs[p], 1024), h2T_dst, t_h2,
                    pe_pre=[LAST["tr34_evac_dv"]], eng="scalar")
                toks[("trH", seq, tb)] = t_trh2
                LAST["tr34_evac_dv"] = t_h2T
                toks[("h2T", seq, tb)] = t_h2T

        for seq in range(SEQ_PER_CORE):
            p1_xload(seq, 0)
            p1_ln1_stats(seq, 0)
            p1_ft(seq, 0)
            p1_ln1_apply(seq, 0)
            for k in range(TBS + 1):
                p1_iter(seq, k)
            toks[("p1done", seq)] = CNT["sDV"]

            if seq == 1:
                @on("gpsimd")
                def _(g, seq=seq):
                    w(g, "sPE", toks[("ty", 0, TBS - 1)])
                    inc("sWv", g.dma_start(
                        out=wvp.ap(),
                        in_=bass.AP(wvd.tensor, 0, [[1024, 128], [131072, 8], [1, 1024]])), 16)
                toks[("wvload", seq)] = CNT["sWv"]

            for tb in range(TBS):
                pair = tb % 3
                @on("tensor")
                def _(t, tb=tb, pair=pair, seq=seq):
                    tk2 = toks[("h2T", seq, tb)]
                    w(t, tk2[0], tk2[1])
                    w(t, "sWv", toks[("wvload", seq)])
                    tkf = toks.get(("vfree", seq, tb - 3))
                    if tkf is not None:
                        w(t, "sDV", tkf)
                    if tb == 0:
                        w(t, "sDV", toks[("p1done", seq)])
                        tk3 = toks[("h2T", seq, TBS - 1)]
                        w(t, tk3[0], tk3[1])
                    for c in range(8):
                        st = (c == 0)
                        t.matmul(PS[:, 1024 * pair:1024 * pair + 512],
                                 h2T.ap()[:, 2048 * c + 128 * tb:2048 * c + 128 * (tb + 1)],
                                 wvp.ap()[:, 1024 * c:1024 * c + 512],
                                 start=st, stop=(c == 7))
                        mm = t.matmul(PS[:, 1024 * pair + 512:1024 * (pair + 1)],
                                      h2T.ap()[:, 2048 * c + 128 * tb:2048 * c + 128 * (tb + 1)],
                                      wvp.ap()[:, 1024 * c + 512:1024 * (c + 1)],
                                      start=st, stop=(c == 7))
                    incd("sPE", "tensor", t, mm)
                t_vps = CNT["sPE"]

                @on("vector")
                def _(v, tb=tb, pair=pair, t_vps=t_vps):
                    w(v, "sPE", t_vps)
                    incd("sDV", "vector", v, v.tensor_copy(
                        bass.AP(vhat, 1040 * tb, [[vhat.shape[1], 128], [65, 16], [1, 64]]),
                        PS[:, 1024 * pair:1024 * (pair + 1)].rearrange(
                            "p (h d) -> p h d", d=64)))
                toks[("vfree", seq, tb)] = CNT["sDV"]
            toks[("vdone", seq)] = CNT["sDV"]

            @on("gpsimd")
            def _(g, seq=seq):
                w(g, "sDV", toks[("vdone", seq)])
                inc("sWv", g.dma_start(
                    out=wvp.ap(),
                    in_=bass.AP(wpd.tensor, 0, [[1024, 128], [131072, 8], [1, 1024]])), 16)
            toks[("wpload", seq)] = CNT["sWv"]

            for hp in range(HP):
                par = hp % 2
                wqo = A2_WQ0 if par == 0 else A2_WQ1
                wko = A2_WK0 if par == 0 else A2_WK1
                sWq = "sWq0" if par == 0 else "sWq1"

                @on("gpsimd")
                def _(g, hp=hp, wqo=wqo, wko=wko, seq=seq, sWq=sWq):
                    tkf = toks.get(("qkmmdone", seq, hp - 2))
                    if tkf is not None:
                        w(g, "sPE", tkf)
                    if hp < 2:
                        w(g, "sDV", toks[("p1done", seq)])
                    if seq == 1 and hp == 0:
                        w(g, "sPE", toks[("qkmmdone", 0, HP - 1)])
                    inc(sWq, g.dma_start(
                        out=abf(wqo, 1024),
                        in_=bass.AP(wqd.tensor, 128 * hp, [[1024, 128], [131072, 8], [1, 128]])), 16)
                    inc(sWq, g.dma_start(
                        out=abf(wko, 1024),
                        in_=bass.AP(wkd.tensor, 128 * hp, [[1024, 128], [131072, 8], [1, 128]])), 16)
                toks[("wqkload", seq, hp)] = CNT[sWq]

                @on("tensor")
                def _(t, hp=hp, wqo=wqo, wko=wko, seq=seq, sWq=sWq):
                    w(t, sWq, toks[("wqkload", seq, hp)])
                    if hp == 0:
                        w(t, "sDV", toks[("vdone", seq)])
                    else:
                        w(t, "sDV", toks[("qkevac", seq, hp - 1)])
                        w(t, "sAC", toks[("kevac", seq, hp - 1)])
                        w(t, "sDV", toks[("otevac", seq, hp - 1, QBS - 1)])
                    for c in range(8):
                        st = (c == 0)
                        for t5 in range(4):
                            t.matmul(PS[:, 512 * t5:512 * (t5 + 1)],
                                     abf(wqo, 1024)[:, 128 * c:128 * (c + 1)],
                                     h2T.ap()[:, 2048 * c + 512 * t5:2048 * c + 512 * (t5 + 1)],
                                     start=st, stop=(c == 7))
                        for t5 in range(4):
                            mm = t.matmul(PS[:, 2048 + 512 * t5:2048 + 512 * (t5 + 1)],
                                          abf(wko, 1024)[:, 128 * c:128 * (c + 1)],
                                          h2T.ap()[:, 2048 * c + 512 * t5:2048 * c + 512 * (t5 + 1)],
                                          start=st, stop=(c == 7))
                    incd("sPE", "tensor", t, mm)
                toks[("qkmmdone", seq, hp)] = CNT["sPE"]

                @on("vector")
                def _(v, hp=hp, seq=seq):
                    w(v, "sPE", toks[("qkmmdone", seq, hp)])
                    incd("sDV", "vector", v, v.tensor_copy(abf(A2_QT, 2048), PS[:, 0:2048]))
                toks[("qkevac", seq, hp)] = CNT["sDV"]

                @on("scalar")
                def _(s, hp=hp, seq=seq):
                    w(s, "sPE", toks[("qkmmdone", seq, hp)])
                    incd("sAC", "scalar", s, s.activation(
                        out=abf(A2_KT, 2048), in_=PS[:, 2048:4096],
                        func=AF.Identity))
                toks[("kevac", seq, hp)] = CNT["sAC"]

                for qb in range(QBS):
                    ntile = 2 * KBS
                    groups = [list(range(3 * i, min(3 * i + 3, ntile)))
                              for i in range((ntile + 2) // 3)]
                    exp_tok = {}
                    avg_tok = {}

                    def ptA(r):
                        return (abf(A2_PT0, 1536) if r == 0 else
                                abf(A2_PT1, 1536) if r == 1 else
                                hh3.ap()[:, 0:768].bitcast(BF16))

                    def emit_scores(gi, gr):
                        rpar = gi % 2

                        @on("tensor")
                        def _(t, hp=hp, qb=qb, gi=gi, gr=gr, seq=seq, rpar=rpar):
                            if gi == 0:
                                w(t, "sDV", toks[("qkevac", seq, hp)])
                                w(t, "sAC", toks[("kevac", seq, hp)])
                            if gi >= 2:
                                w(t, "sAC", exp_tok[gi - 2])
                            for j, ti in enumerate(gr):
                                head = ti % 2
                                kb = ti // 2
                                slot = 3 * rpar + j
                                mm = t.matmul(
                                    PS[:, 512 * slot:512 * (slot + 1)],
                                    abf(A2_KT, 2048, 64 * head, 64 * head + 64)[:, 128 * kb:128 * (kb + 1)],
                                    abf(A2_QT, 2048, 64 * head, 64 * head + 64)[:, 512 * qb:512 * (qb + 1)],
                                    start=True, stop=True)
                            incd("sPE", "tensor", t, mm)
                        toks[("sc", seq, hp, qb, gi)] = CNT["sPE"]

                        @on_raw("scalar")
                        def _(s, hp=hp, qb=qb, gi=gi, gr=gr, seq=seq, rpar=rpar):
                            w(s, "sPE", toks[("sc", seq, hp, qb, gi)])
                            if gi >= 3:
                                w(s, "sPE", avg_tok[gi - 3])
                            ncols = 512 * len(gr)
                            incd("sAC", "scalar", s, s.activation(
                                out=ptA(gi % 3)[:, 0:ncols],
                                in_=PS[:, 1536 * rpar:1536 * rpar + ncols],
                                func=AF.Exp,
                                scale=cpool[:, 0:1], bias=cpool[:, 1:2]))
                        exp_tok[gi] = CNT["sAC"]

                    def emit_av(gi, gr):
                        rpar = gi % 2

                        @on("tensor")
                        def _(t, hp=hp, qb=qb, gi=gi, gr=gr, seq=seq, rpar=rpar):
                            w(t, "sAC", exp_tok[gi])
                            if gi == 0:
                                if qb == 0:
                                    tk = toks.get(("otevac", seq, hp - 1, QBS - 1))
                                else:
                                    tk = toks.get(("otevac", seq, hp, qb - 1))
                                if tk is not None:
                                    w(t, "sDV", tk)
                            pto = ptA(gi % 3)
                            for j, ti in enumerate(gr):
                                head = ti % 2
                                kb = ti // 2
                                mm = t.matmul(
                                    PS[0:65, 3072 + 512 * head:3072 + 512 * head + 512],
                                    vhat.ap()[:, 1040 * kb + 65 * (2 * hp + head):
                                              1040 * kb + 65 * (2 * hp + head) + 65],
                                    pto[:, 512 * j:512 * (j + 1)],
                                    start=(kb == 0), stop=(kb == KBS - 1))
                            incd("sPE", "tensor", t, mm)
                        avg_tok[gi] = CNT["sPE"]

                    for gi, gr in enumerate(groups):
                        emit_scores(gi, gr)
                        if gi >= 1:
                            emit_av(gi - 1, groups[gi - 1])
                    emit_av(len(groups) - 1, groups[-1])
                    toks[("av", seq, hp, qb)] = avg_tok[len(groups) - 1]

                    @on_raw("vector")
                    def _(v, hp=hp, qb=qb, seq=seq):
                        w(v, "sPE", toks[("av", seq, hp, qb)])
                        if LAST.get("bstg_free") is not None:
                            w(v, "sBm", LAST["bstg_free"])
                        if LAST.get("scr_free") is not None:
                            w(v, "sGP", LAST["scr_free"])
                        if LAST.get("reca_free") is not None:
                            w(v, "sGP", LAST["reca_free"])
                        v.tensor_copy(
                            arena.ap()[0:64, A2_OTF + 512 * qb:A2_OTF + 512 * (qb + 1)],
                            PS[0:64, 3072:3584])
                        v.tensor_copy(bstg[:, :], PS[0:64, 3584:4096])
                        v.tensor_copy(
                            arena.ap()[0:1, A2_RECB + 512 * qb:A2_RECB + 512 * (qb + 1)],
                            PS[64:65, 3072:3584])
                        incd("sDV", "vector", v, v.tensor_copy(arena.ap()[0:1, A2_SCR:A2_SCR + 512],
                                                 PS[64:65, 3584:4096]))
                    toks[("otevac", seq, hp, qb)] = CNT["sDV"]

                    @on("sync")
                    def _(sy, hp=hp, qb=qb, seq=seq):
                        w(sy, "sDV", toks[("otevac", seq, hp, qb)])
                        inc("sBm", sy.dma_start(
                            out=arena.ap()[64:128, A2_OTF + 512 * qb:A2_OTF + 512 * (qb + 1)],
                            in_=bstg[:, :]), 16)
                    LAST["bstg_free"] = CNT["sBm"]
                    toks[("bdma", seq, hp, qb)] = CNT["sBm"]

                    @on("gpsimd")
                    def _(g, hp=hp, qb=qb, seq=seq):
                        w(g, "sDV", toks[("otevac", seq, hp, qb)])
                        if hp == 0:
                            w(g, "sSP")
                            w(g, "sOU")
                        if LAST.get("reca_free") is not None:
                            w(g, "sGP", LAST["reca_free"])
                        inc("sGP", g.partition_broadcast(
                            arena.ap()[0:128, A2_SUMB + 512 * qb:A2_SUMB + 512 * (qb + 1)],
                            arena.ap()[0:1, A2_SCR:A2_SCR + 512], channels=128))
                    LAST["scr_free"] = CNT["sGP"]
                    toks[("bsumb", seq, hp, qb)] = CNT["sGP"]

                @on("gpsimd")
                def _(g, hp=hp, seq=seq):
                    w(g, "sDV", toks[("otevac", seq, hp, QBS - 1)])
                    w(g, "sGP", toks[("bsumb", seq, hp, QBS - 1)])
                    inc("sGP", g.partition_broadcast(af(A2_SUMB, 2048, 0, 64),
                                          arena.ap()[0:1, A2_RECB:A2_RECB + 2048],
                                          channels=64))
                toks[("bsuma", seq, hp)] = CNT["sGP"]
                LAST["reca_free"] = CNT["sGP"]

                @on("vector")
                def _(v, hp=hp, seq=seq):
                    w(v, "sGP", toks[("bsuma", seq, hp)])
                    w(v, "sBm", toks[("bdma", seq, hp, QBS - 1)])
                    inc("sDV", v.reciprocal(af(A2_RECB, 2048), af(A2_SUMB, 2048)))
                    w(v, "sDV")
                    inc("sDV", v.tensor_tensor(
                        out=oT.ap()[:, 2048 * hp:2048 * (hp + 1)],
                        in0=af(A2_OTF, 2048), in1=af(A2_RECB, 2048),
                        op=ALU.mult))
                toks[("mixdone", seq, hp)] = CNT["sDV"]
            toks[("p2done", seq)] = CNT["sDV"]

            H2R3s = [A1_H2R, A1_X0]
            FT3s = [A1_FT, A1_X1]
            HT3s = [A1_HT, 11776]
            LAST["hl_read_pe"] = toks[("trT", seq, TBS - 2)]
            toks[("ffngl3", seq, -2)] = toks[("ffngl", seq, TBS - 2)]
            toks[("ffnout3", seq, -1)] = toks[("ffnout", seq, TBS - 2)]

            def p3_r1load(k):
                gtb = seq * TBS + k
                rslot = A1_R10 if k % 2 == 0 else A1_R11
                sR = "sRa" if k % 2 == 0 else "sRb"

                @on("sync")
                def _(sy):
                    tkf = toks.get(("r1free", seq, k - 2))
                    if tkf is not None:
                        w(sy, "sDV", tkf)
                    if k == 0:
                        w(sy, "sDV", toks[("p2done", seq)])
                        w(sy, "sSP")
                    inc(sR, sy.dma_start(out=af(rslot, 1024),
                                         in_=r1scr[128 * gtb:128 * (gtb + 1), :]), 16)
                toks[("r1load", seq, k)] = CNT[sR]

            def p3_pre(k):
                p = k % 2
                rslot = A1_R10 if p == 0 else A1_R11
                sR = "sRa" if p == 0 else "sRb"

                @on("vector")
                def _(v):
                    w(v, sR, toks[("r1load", seq, k)])
                    inc("sDV", v.tensor_scalar(
                        out=af(H2R3s[p], 1024), in0=af(rslot, 1024),
                        scalar1=ln2m[:, k:k + 1], scalar2=ln2r[:, k:k + 1],
                        op0=ALU.subtract, op1=ALU.mult))
                    w(v, "sDV")
                    v.tensor_add(af(H2R3s[p], 1024), af(H2R3s[p], 1024),
                                 bprojb[:, :])
                    inc("sDV", v.tensor_add(af(FT3s[p], 1024), betab[:, :],
                                            af(rslot, 1024)))
                toks[("r1free", seq, k)] = CNT["sDV"]

            def p3_proj(k):
                @on("tensor")
                def _(t):
                    w(t, "sDV", toks[("mixdone", seq, HP - 1)])
                    w(t, "sWv", toks[("wpload", seq)])
                    tky = toks.get(("pyread", seq, k - 1))
                    if tky is not None:
                        w(t, "sDV", tky)
                    for hp in range(HP):
                        st = (hp == 0)
                        t.matmul(PS[:, 3072:3584],
                                 oT.ap()[:, 2048 * hp + 128 * k:2048 * hp + 128 * (k + 1)],
                                 wvp.ap()[:, 1024 * hp:1024 * hp + 512],
                                 start=st, stop=(hp == 7))
                        mm = t.matmul(PS[:, 3584:4096],
                                      oT.ap()[:, 2048 * hp + 128 * k:2048 * hp + 128 * (k + 1)],
                                      wvp.ap()[:, 1024 * hp + 512:1024 * (hp + 1)],
                                      start=st, stop=(hp == 7))
                    incd("sPE", "tensor", t, mm)
                toks[("ty", seq, k)] = CNT["sPE"]

            def p3_yread(k):
                p = k % 2
                sbi4 = LNROT[0] % 4
                LNROT[0] += 1
                bnst4, mv24 = bnstI(sbi4), mv2I(sbi4)
                rstd4, nb4 = rstdI(sbi4), nbI(sbi4)

                @on("vector")
                def _(v):
                    w(v, "sPE", toks[("ty", seq, k)])
                    tkx = toks.get(("pxh", seq, k - 1))
                    if tkx is not None:
                        w(v, "sAC", tkx)
                    inc("sDV", v.scalar_tensor_tensor(
                        out=af(A1_Y, 1024), in0=PS[:, 3072:4096],
                        scalar=1.0, in1=af(H2R3s[p], 1024),
                        op0=ALU.mult, op1=ALU.add))
                    toks[("pyread", seq, k)] = CNT["sDV"]
                    w(v, "sDV")
                    v.bn_stats(bnst4[:, 0:6], af(A1_Y, 512))
                    inc("sDV", v.bn_stats(bnst4[:, 6:12], af(A1_Y + 512, 512)))
                    w(v, "sDV")
                    inc("sDV", v.bn_aggr(mv24[:, :], bnst4[:, 0:12]))
                t_st4 = CNT["sDV"]

                @on("scalar")
                def _(s):
                    w(s, "sDV", t_st4)
                    inc("sAC", s.activation(out=rstd4[:, :], in_=mv24[:, 1:2],
                                            func=AF.Ln, bias=cpool[:, 3:4]))
                    w(s, "sAC")
                    inc("sAC", s.activation(out=rstd4[:, :], in_=rstd4[:, :],
                                            func=AF.Exp, scale=cpool[:, 4:5]))
                t_r4 = CNT["sAC"]

                @on("vector")
                def _(v):
                    w(v, "sAC", t_r4)
                    inc("sDV", v.scalar_tensor_tensor(
                        out=nb4[:, :], in0=mv24[:, 0:1], scalar=-1.0,
                        in1=rstd4[:, :], op0=ALU.mult, op1=ALU.mult))
                t_nb = CNT["sDV"]

                @on("scalar")
                def _(s):
                    w(s, "sDV", t_nb)
                    inc("sAC", s.activation(out=af(A1_XH, 1024), in_=af(A1_Y, 1024),
                                            func=AF.Identity,
                                            scale=rstd4[:, :], bias=nb4[:, :]))
                toks[("pxh", seq, k)] = CNT["sAC"]

            def p3_tail(k):
                p = k % 2

                @on("vector")
                def _(v):
                    w(v, "sAC", toks[("pxh", seq, k)])
                    inc("sDV", v.tensor_mul(af(A1_XH, 1024), af(A1_XH, 1024),
                                            gammab[:, :]))
                    w(v, "sDV")
                    inc("sDV", v.tensor_add(af(A1_R2, 1024), af(A1_XH, 1024),
                                            af(FT3s[p], 1024)))
                t_r2 = CNT["sDV"]

                t_h3 = emit_ln(af(A1_R2, 1024), af(A1_HL, 1024), 2,
                               pre=[("sDV", t_r2)],
                               gap=lambda v: v.tensor_add(
                                   hh3.ap()[:, 1024 * p:1024 * (p + 1)],
                                   halfb[:, :], af(A1_R2, 1024)),
                               apply_pre=[("sPE", LAST["hl_read_pe"])])

                t_tr3 = emit_tr_pe(af(A1_HL, 1024), t_h3,
                                   pe_pre=[LAST["tr34_evac_dv"]])
                LAST["hl_read_pe"] = t_tr3

                evac_pre = []
                tk = toks.get(("ffngl3", seq, k - 2))
                if tk is not None:
                    evac_pre.append(("sPE", tk))
                t_hT3 = emit_tr_evac(abf(HT3s[p], 1024), t_tr3, evac_pre=evac_pre,
                                     eng="scalar")
                LAST["tr34_evac_dv"] = t_hT3
                toks[("hT3", seq, k)] = t_hT3

            def p3_ffn(k):
                p = k % 2
                pe_pre = []
                tk = toks.get(("ffnout3", seq, k - 1))
                if tk is not None:
                    pe_pre.append(("sDV", tk))
                t_gl = emit_ffn_mm(abf(HT3s[p], 1024), toks[("hT3", seq, k)],
                                   par=0, pe_pre=pe_pre)
                toks[("ffngl3", seq, k)] = t_gl

            def p3_out(k):
                p = k % 2
                gtb = seq * TBS + k
                t_tsc2 = emit_ffn_gate(toks[("ffngl3", seq, k)], par=0)
                t_out = emit_ffn_tail2(af(A1_OUTF, 1024),
                                       hh3.ap()[:, 1024 * p:1024 * (p + 1)],
                                       t_tsc2, par=0,
                                       last_pre=[LAST["outf_read"]])
                toks[("ffnout3", seq, k)] = t_out

                @on("sync")
                def _(sy):
                    w(sy, "sDV", t_out)
                    inc("sOU", sy.dma_start(out=outd[128 * gtb:128 * (gtb + 1), :],
                                            in_=af(A1_OUTF, 1024)), 16)
                LAST["outf_read"] = ("sOU", CNT["sOU"])

            p3_r1load(0)
            p3_pre(0)
            p3_proj(0)
            for k in range(TBS + 2):
                if k < TBS:
                    p3_yread(k)
                if k + 1 < TBS:
                    p3_r1load(k + 1)
                if 2 <= k:
                    p3_out(k - 2)
                if k + 1 < TBS:
                    p3_pre(k + 1)
                    p3_proj(k + 1)
                if k < TBS:
                    p3_tail(k)
                if 1 <= k <= TBS:
                    p3_ffn(k - 1)
            toks[("p3done", seq)] = CNT["sDV"]

        @on("sync")
        def _(sy):
            w(sy, "sOU")
            w(sy, "sSP")
            w(sy, "sBm")

    es.close()
    mybir.codegen_inst_isa_subclasses(nc)
    return nc


_NC_CACHE = None


def kernel(L0_attn_x, L0_attn_w_q, L0_attn_w_k, L0_attn_w_v, L0_attn_w_proj,
           L0_attn_b_proj, L0_attn_ln_gamma, L0_attn_ln_beta,
           L0_ffn_W_lin, L0_ffn_b_lin, L0_ffn_gate_w, L0_ffn_left, L0_ffn_right):
    global _NC_CACHE
    if _NC_CACHE is None:
        _NC_CACHE = build_nc()
    nc = _NC_CACHE

    f = np.float32
    x = np.ascontiguousarray(np.asarray(L0_attn_x, dtype=f))
    wq = np.ascontiguousarray(np.asarray(L0_attn_w_q, dtype=f))
    wk = np.ascontiguousarray(np.asarray(L0_attn_w_k, dtype=f))
    wv = np.ascontiguousarray(np.asarray(L0_attn_w_v, dtype=f))
    wp = np.ascontiguousarray(np.asarray(L0_attn_w_proj, dtype=f))
    wlt = np.ascontiguousarray(np.asarray(L0_ffn_W_lin, dtype=f).T)
    gl = np.ascontiguousarray(np.concatenate(
        [np.asarray(L0_ffn_gate_w, dtype=f).T,
         np.asarray(L0_ffn_left, dtype=f).transpose(1, 0, 2).reshape(D, 32)], axis=1))
    rgtv = np.ascontiguousarray(np.asarray(L0_ffn_right, dtype=f).reshape(32, D))
    vecs = np.ascontiguousarray(np.stack(
        [np.asarray(L0_attn_b_proj, dtype=f),
         np.asarray(L0_ffn_b_lin, dtype=f),
         np.asarray(L0_attn_ln_gamma, dtype=f),
         np.asarray(L0_attn_ln_beta, dtype=f)], axis=0))

    in_maps = []
    for c in range(NCORES):
        xs = np.ascontiguousarray(
            x[SEQ_PER_CORE * c:SEQ_PER_CORE * (c + 1)].reshape(TOK, D))
        in_maps.append({"x": xs, "wq": wq, "wk": wk, "wv": wv, "wp": wp,
                        "wlt": wlt, "gld": gl, "rgt": rgtv, "vec": vecs})

    res = run_bass_kernel_spmd(nc, in_maps, core_ids=list(range(NCORES)))
    out = np.concatenate(
        [r["out"].reshape(SEQ_PER_CORE, S, D) for r in res.results], axis=0)
    return out.astype(np.float32)


# version 21 (final)
# speedup vs baseline: 1.0140x; 1.0140x over previous
import numpy as np
import concourse.bass as bass
import concourse.mybir as mybir
from concourse import library_config
from concourse.bass_utils import run_bass_kernel_spmd
from contextlib import ExitStack

F32 = mybir.dt.float32
BF16 = mybir.dt.bfloat16
AF = mybir.ActivationFunctionType
ALU = mybir.AluOpType
AX = mybir.AxisListType

NCORES = 8
B, S, D = 16, 2048, 1024
SEQ_PER_CORE = B // NCORES
TOK = SEQ_PER_CORE * S
TBS = S // 128
HP = 8
QBS = 4
KBS = 16
EPS_LN = 1e-5
EPS_LN4 = 1e-6

AR_N = 12288
A2_QT = 0
A2_KT = 1024
A2_WQ0 = 2048
A2_WQ1 = 2560
A2_WK0 = 3072
A2_WK1 = 3584
A2_WV = 4096
A2_PT0 = 4096
A2_PT1 = 4864
A2_SCR = 5632
A2_OTF = 6144
A2_SUMB = 8192
A2_RECB = 10240
A1_X0 = 0
A1_X1 = 1024
A1_HL = 2048
A1_HT = 3072
A1_R10 = 3584
A1_R11 = 4608
A1_Y = 5632
A1_XH = 6656
A1_R2 = 7680
A1_OUTF = 8704
A1_FT = 9728
A1_H2R = 10752
HLs = [A1_HL, A1_R10]
HTs = [A1_HT, A1_R11]
FTs = [A1_FT, A1_Y]
H2Rs = [A1_H2R, A1_XH]
PBs = [0, 3072]
GLs = [1024, 2688]


def build_nc():
    nc = bass.Bass("TRN2", target_bir_lowering=False)

    xd = nc.dram_tensor("x", [TOK, D], F32, kind="ExternalInput").ap()
    wqd = nc.dram_tensor("wq", [D, D], F32, kind="ExternalInput").ap()
    wkd = nc.dram_tensor("wk", [D, D], F32, kind="ExternalInput").ap()
    wvd = nc.dram_tensor("wv", [D, D], F32, kind="ExternalInput").ap()
    wpd = nc.dram_tensor("wp", [D, D], F32, kind="ExternalInput").ap()
    wlt = nc.dram_tensor("wlt", [D, D], F32, kind="ExternalInput").ap()
    gld = nc.dram_tensor("gld", [D, 40], F32, kind="ExternalInput").ap()
    rgt = nc.dram_tensor("rgt", [32, D], F32, kind="ExternalInput").ap()
    vec = nc.dram_tensor("vec", [4, D], F32, kind="ExternalInput").ap()
    outd = nc.dram_tensor("out", [TOK, D], F32, kind="ExternalOutput").ap()
    r1scr = nc.dram_tensor("r1scr", [TOK, D], F32, kind="Internal").ap()
    identd = nc.inline_tensor(np.eye(128, dtype=np.float32), "identd").ap()

    es = ExitStack()
    sb = lambda n, s, d: es.enter_context(nc.sbuf_tensor(n, s, d))

    wlin = sb("wlin", [128, 8 * 1024], BF16)
    wvp = sb("wvp", [128, 8 * 1024], BF16)
    glsb = sb("glsb", [128, 8 * 40], BF16)
    rgts = sb("rgts", [128, 1024], BF16)
    bprojb = sb("bprojb", [128, 1024], F32)
    halfb = sb("halfb", [128, 1024], F32)
    gammab = sb("gammab", [128, 1024], F32)
    betab = sb("betab", [128, 1024], F32)
    idt = sb("idt", [128, 128], F32)
    cpool = sb("cpool", [128, 8], F32)
    h2T = sb("h2T", [128, 8 * 2048], BF16)
    oT = sb("oT", [128, 8 * 2048], BF16)
    vhat = sb("vhat", [128, KBS * 1040], BF16)
    bstg = sb("bstg", [64, 512], F32)
    ln2m = sb("ln2m", [128, TBS], F32)
    ln2r = sb("ln2r", [128, TBS], F32)
    tTs = sb("tTs", [128, 128], BF16)
    sce_b = sb("sce", [128, 16], F32)
    top8_b = sb("top8", [128, 16], F32)
    tmp8_b = sb("tmp8", [128, 16], F32)
    tsc_b = sb("tsc", [128, 64], F32)
    ssum_b = sb("ssum", [128, 2], F32)
    rs8_b = sb("rs8", [128, 2], F32)
    hh3 = sb("hh3", [128, 2048], F32)
    bnst_b = sb("bnst", [128, 48], F32)
    mv2_b = sb("mv2", [128, 8], F32)
    rstd_b = sb("rstd", [128, 4], F32)

    def _sl(buf, i, n):
        return buf.ap()[:, n * i:n * (i + 1)]
    sceI = lambda p: _sl(sce_b, p, 8)
    top8I = lambda p: _sl(top8_b, p, 8)
    tmp8I = lambda p: _sl(tmp8_b, p, 8)
    tscI = lambda p: _sl(tsc_b, p, 32)
    ssumI = lambda p: _sl(ssum_b, p, 1)
    rs8I = lambda p: _sl(rs8_b, p, 1)
    bnstI = lambda i: _sl(bnst_b, i, 12)
    mv2I = lambda i: _sl(mv2_b, i, 2)
    rstdI = lambda i: _sl(rstd_b, i, 1)
    nbI = lambda i: _sl(bnst_b, i, 12)[:, 0:1]
    arena = sb("arena", [128, AR_N], F32)
    PS = es.enter_context(nc.psum_tensor("PS", [128, 4096], F32))

    def abf(off_f32, n_bf16, p0=0, p1=128):
        return arena.ap()[p0:p1, off_f32:off_f32 + n_bf16 // 2].bitcast(BF16)

    def af(off, n, p0=0, p1=128):
        return arena.ap()[p0:p1, off:off + n]

    semnames = ["sWc", "sWv", "sWq0", "sWq1", "sXa", "sXb", "sRa", "sRb",
                "sIN", "sSP", "sOU", "sBm", "sPE", "sDV", "sAC", "sGP"]
    SEM = {n: es.enter_context(nc.semaphore(n)) for n in semnames}
    CNT = {n: 0 for n in semnames}

    DRAIN_SET = set()

    def inc(name, instr, k=1):
        instr.then_inc(SEM[name], k)
        CNT[name] += k
        return CNT[name]

    def incd(name, engname, eng, instr, k=1):
        if engname in DRAIN_SET:
            return inc(name, eng.drain(), k)
        return inc(name, instr, k)

    def w(eng, name, tok=None):
        eng.wait_ge(SEM[name], CNT[name] if tok is None else tok)

    BLK = {}

    def on_raw(engname):
        def deco(f):
            getattr(BLK["b"], engname)(f)
        return deco

    def on(engname):
        def deco(f):
            if engname == "vector":
                def g(e):
                    e.wait_ge(SEM["sDV"], CNT["sDV"])
                    f(e)
            elif engname == "scalar":
                def g(e):
                    e.wait_ge(SEM["sAC"], CNT["sAC"])
                    f(e)
            else:
                g = f
            getattr(BLK["b"], engname)(g)
        return deco

    toks = {}
    LAST = {}

    with nc.Block() as block:
        BLK["b"] = block

        PRELOAD_ACT_TABLE = True

        @on("scalar")
        def _(s):
            if not PRELOAD_ACT_TABLE:
                return
            try:
                from concourse.hw_specs import get_activation_tables
                idx = None
                for i2, funcs in enumerate(
                        get_activation_tables(s.bass.m.arch).values()):
                    if AF.Ln in funcs and AF.Exp in funcs and AF.Identity in funcs:
                        idx = i2
                        break
                if idx is not None:
                    s.add_instruction(mybir.InstLoadActFuncSet(
                        name=s.bass.get_next_instruction_name(),
                        act_func_set_id=idx, ins=[], outs=[]))
            except Exception:
                pass

        @on("sync")
        def _(sy):
            inc("sIN", sy.dma_start(out=idt[:, :], in_=identd[:, :]), 16)
            for v in range(4):
                inc("sIN", sy.dma_start(out=arena.ap()[0:1, 1024 * v:1024 * v + 1024],
                                        in_=vec[v:v + 1, :]), 16)
        TOK_VEC = CNT["sIN"]

        @on("vector")
        def _(v):
            for i, val in enumerate([0.125, -20.0, EPS_LN, EPS_LN4, -0.5, 1.0]):
                v.memset(cpool[:, i:i + 1], val)
            v.memset(rgts.ap()[32:64, :], 0.0)
            v.memset(rgts.ap()[64:128, :], 0.0)
            v.memset(tTs.ap()[32:64, :], 0.0)
            v.memset(tTs.ap()[64:128, :], 0.0)
            incd("sDV", "vector", v, v.memset(
                bass.AP(vhat, 64, [[vhat.shape[1], 128], [1040, KBS], [65, 16], [1, 1]]), 1.0))
        TOK_MEMSET = CNT["sDV"]

        @on("gpsimd")
        def _(g):
            g.load_library(library_config.attn)
            for c in range(8):
                inc("sWc", g.dma_start(out=wlin.ap()[:, 1024 * c:1024 * (c + 1)],
                                       in_=wlt[128 * c:128 * (c + 1), :]), 16)
                inc("sWc", g.dma_start(out=glsb.ap()[:, 40 * c:40 * (c + 1)],
                                       in_=gld[128 * c:128 * (c + 1), :]), 16)
            inc("sWv", g.dma_start(
                out=wvp.ap(),
                in_=bass.AP(wvd.tensor, 0, [[1024, 128], [131072, 8], [1, 1024]])), 16)
            w(g, "sDV", TOK_MEMSET)
            inc("sWc", g.dma_start(out=rgts.ap()[0:32, :], in_=rgt[:, :]), 16)
            w(g, "sIN", TOK_VEC)
            inc("sGP", g.partition_broadcast(bprojb[:, :], arena.ap()[0:1, 0:1024], channels=128))
            inc("sGP", g.partition_broadcast(halfb[:, :], arena.ap()[0:1, 1024:2048], channels=128))
            inc("sGP", g.partition_broadcast(gammab[:, :], arena.ap()[0:1, 2048:3072], channels=128))
            inc("sGP", g.partition_broadcast(betab[:, :], arena.ap()[0:1, 3072:4096], channels=128))
        TOK_WALL = CNT["sWc"]
        TOK_BCV = CNT["sGP"]
        toks[("wvload", 0)] = CNT["sWv"]

        @on("vector")
        def _(v):
            w(v, "sGP", TOK_BCV)
            incd("sDV", "vector", v, v.tensor_scalar_mul(halfb[:, :], halfb[:, :], 0.5))
        TOK_CONSTS = CNT["sDV"]

        LNROT = [0]

        def emit_ln_stats(src_ap, eps_col, pre=(), gap=None):
            sbi = LNROT[0] % 4
            LNROT[0] += 1
            bnst, mv2, rstd = bnstI(sbi), mv2I(sbi), rstdI(sbi)

            @on("vector")
            def _(v):
                for nm, tk in pre:
                    w(v, nm, tk)
                v.bn_stats(bnst[:, 0:6], src_ap[:, 0:512])
                last = v.bn_stats(bnst[:, 6:12], src_ap[:, 512:1024])
                if gap is not None:
                    last = gap(v)
                inc("sDV", last)
                w(v, "sDV")
                inc("sDV", v.bn_aggr(mv2[:, :], bnst[:, 0:12]))
            t_stats = CNT["sDV"]

            @on("scalar")
            def _(s):
                w(s, "sDV", t_stats)
                inc("sAC", s.activation(out=rstd[:, :], in_=mv2[:, 1:2], func=AF.Ln,
                                        bias=cpool[:, eps_col:eps_col + 1]))
                w(s, "sAC")
                inc("sAC", s.activation(out=rstd[:, :], in_=rstd[:, :], func=AF.Exp,
                                        scale=cpool[:, 4:5]))
            return sbi, CNT["sAC"]

        def emit_ln_apply(src_ap, dst_ap, sbi, t_rstd, apply_pre=(),
                          store_tb=None):
            mv2, rstd = mv2I(sbi), rstdI(sbi)

            @on("vector")
            def _(v):
                w(v, "sAC", t_rstd)
                for nm, tk in apply_pre:
                    w(v, nm, tk)
                last = v.tensor_scalar(out=dst_ap, in0=src_ap, scalar1=mv2[:, 0:1],
                                       scalar2=rstd[:, :], op0=ALU.subtract, op1=ALU.mult)
                if store_tb is not None:
                    v.tensor_copy(ln2m[:, store_tb:store_tb + 1], mv2[:, 0:1])
                    last = v.tensor_copy(ln2r[:, store_tb:store_tb + 1], rstd[:, :])
                inc("sDV", last)
            return CNT["sDV"]

        def emit_ln(src_ap, dst_ap, eps_col, pre=(), store_tb=None, gap=None,
                    apply_pre=()):
            sbi, t_rstd = emit_ln_stats(src_ap, eps_col, pre=pre, gap=gap)
            return emit_ln_apply(src_ap, dst_ap, sbi, t_rstd,
                                 apply_pre=apply_pre, store_tb=store_tb)

        def emit_tr_pe(src_cols_ap, tok_src_dv, pe_pre=()):
            @on("tensor")
            def _(t):
                w(t, "sDV", tok_src_dv)
                for nm, tk in pe_pre:
                    w(t, nm, tk)
                for c in range(7):
                    t.transpose(PS[:, 1536 + 128 * c:1536 + 128 * (c + 1)],
                                src_cols_ap[:, 128 * c:128 * (c + 1)], idt[:, :])
                incd("sPE", "tensor", t, t.transpose(PS[:, 2432:2560],
                                       src_cols_ap[:, 896:1024], idt[:, :]))
            return CNT["sPE"]

        def emit_tr_evac(dst_bf16_ap, t_tr, evac_pre=(), eng="vector"):
            if eng == "vector":
                @on("vector")
                def _(v):
                    w(v, "sPE", t_tr)
                    for nm, tk in evac_pre:
                        w(v, nm, tk)
                    inc("sDV", v.tensor_copy(dst_bf16_ap, PS[:, 1536:2560]))
                return ("sDV", CNT["sDV"])

            @on("scalar")
            def _(s):
                w(s, "sPE", t_tr)
                for nm, tk in evac_pre:
                    w(s, nm, tk)
                inc("sAC", s.activation(out=dst_bf16_ap, in_=PS[:, 1536:2560],
                                        func=AF.Identity))
            return ("sAC", CNT["sAC"])

        def emit_transpose_block(src_cols_ap, dst_bf16_ap, tok_src_dv,
                                 pe_pre=(), evac_pre=(), eng="vector"):
            t_tr = emit_tr_pe(src_cols_ap, tok_src_dv, pe_pre)
            t_ev = emit_tr_evac(dst_bf16_ap, t_tr, evac_pre, eng=eng)
            return t_ev, t_tr

        def emit_ffn_mm(hT_ap, tok_hT_dv, par=0, pe_pre=()):
            pb, gl = PBs[par], GLs[par]

            @on("tensor")
            def _(t):
                w(t, tok_hT_dv[0], tok_hT_dv[1])
                w(t, "sWc", TOK_WALL)
                for nm, tk in pe_pre:
                    w(t, nm, tk)
                for c in range(8):
                    st = (c == 0)
                    t.matmul(PS[:, pb:pb + 512], hT_ap[:, 128 * c:128 * (c + 1)],
                             wlin.ap()[:, 1024 * c:1024 * c + 512], start=st, stop=False)
                    t.matmul(PS[:, pb + 512:pb + 1024], hT_ap[:, 128 * c:128 * (c + 1)],
                             wlin.ap()[:, 1024 * c + 512:1024 * (c + 1)], start=st, stop=False)
                    mm = t.matmul(PS[:, gl:gl + 40], hT_ap[:, 128 * c:128 * (c + 1)],
                                  glsb.ap()[:, 40 * c:40 * (c + 1)], start=st, stop=(c == 7))
                incd("sPE", "tensor", t, mm)
            return CNT["sPE"]

        def emit_ffn_gate(t_gl, par=0):
            pb, gl = PBs[par], GLs[par]
            sce, top8 = sceI(par), top8I(par)
            tmp8, tsc = tmp8I(par), tscI(par)
            ssum, rs8 = ssumI(par), rs8I(par)

            @on_raw("scalar")
            def _(s):
                w(s, "sPE", t_gl)
                incd("sAC", "scalar", s, s.activation(
                    out=sce[:, :], in_=PS[:, gl:gl + 8], func=AF.Exp))
            t_e8 = CNT["sAC"]

            @on("vector")
            def _(v):
                w(v, "sPE", t_gl)
                w(v, "sAC", t_e8)
                v.reduce_sum(ssum[:, :], sce[:, :], axis=AX.X)
                inc("sDV", v.max(out=top8[:, :], in_=sce[:, :]))
                w(v, "sDV")
                v.reciprocal(rs8[:, :], ssum[:, :])
                inc("sDV", v.memset(top8[:, 2:8], 0.0))
                w(v, "sDV")
                inc("sDV", v.match_replace(out=tmp8[:, :], in_to_replace=top8[:, :],
                                           in_values=sce[:, :], imm_value=0.0))
                w(v, "sDV")
                inc("sDV", v.tensor_sub(tmp8[:, :], sce[:, :], tmp8[:, :]))
                w(v, "sDV")
                inc("sDV", v.scalar_tensor_tensor(
                    out=tsc[:, :].rearrange("p (e r) -> p e r", r=4),
                    in0=PS[:, gl + 8:gl + 40].rearrange("p (e r) -> p e r", r=4),
                    scalar=rs8[:, :],
                    in1=tmp8[:, :, None].broadcast_to([128, 8, 4]),
                    op0=ALU.mult, op1=ALU.mult))
            return CNT["sDV"]

        def emit_ffn_tail2(out_ap, ft_ap, t_tsc, par=0, last_pre=()):
            pb, gl = PBs[par], GLs[par]
            tsc = tscI(par)

            @on("tensor")
            def _(t):
                w(t, "sDV", t_tsc)
                tke = LAST.get("tTs_evac_dv")
                if tke is not None:
                    w(t, "sDV", tke)
                incd("sPE", "tensor", t, t.transpose(PS[0:32, 2560:2688], tsc[:, :], idt[:, :]))
            t_tT = CNT["sPE"]

            @on("vector")
            def _(v):
                w(v, "sPE", t_tT)
                incd("sDV", "vector", v, v.tensor_copy(tTs.ap()[0:32, :], PS[0:32, 2560:2688]))
            t_tTe = CNT["sDV"]
            LAST["tTs_evac_dv"] = t_tTe

            @on("tensor")
            def _(t):
                w(t, "sDV", t_tTe)
                t.matmul(PS[:, pb:pb + 512], tTs.ap()[:, :], rgts.ap()[:, 0:512],
                         start=False, stop=True)
                incd("sPE", "tensor", t, t.matmul(PS[:, pb + 512:pb + 1024], tTs.ap()[:, :],
                                    rgts.ap()[:, 512:1024], start=False, stop=True))
            t_base = CNT["sPE"]

            return emit_ffn_last(out_ap, ft_ap, t_base, par=par,
                                 last_pre=last_pre)

        def emit_ffn_lora(t_tsc, par=0):
            pb, gl = PBs[par], GLs[par]
            tsc = tscI(par)

            @on("tensor")
            def _(t):
                w(t, "sDV", t_tsc)
                tke = LAST.get("tTs_evac_dv")
                if tke is not None:
                    w(t, "sDV", tke)
                incd("sPE", "tensor", t, t.transpose(PS[0:32, 2560:2688], tsc[:, :], idt[:, :]))
            t_tT = CNT["sPE"]

            @on("vector")
            def _(v):
                w(v, "sPE", t_tT)
                incd("sDV", "vector", v, v.tensor_copy(tTs.ap()[0:32, :], PS[0:32, 2560:2688]))
            t_tTe = CNT["sDV"]
            LAST["tTs_evac_dv"] = t_tTe

            @on("tensor")
            def _(t):
                w(t, "sDV", t_tTe)
                t.matmul(PS[:, pb:pb + 512], tTs.ap()[:, :], rgts.ap()[:, 0:512],
                         start=False, stop=True)
                incd("sPE", "tensor", t, t.matmul(PS[:, pb + 512:pb + 1024], tTs.ap()[:, :],
                                    rgts.ap()[:, 512:1024], start=False, stop=True))
            return CNT["sPE"]

        def emit_ffn_last(out_ap, ft_ap, t_base, par=0, last_pre=()):
            pb = PBs[par]

            @on("vector")
            def _(v):
                w(v, "sPE", t_base)
                for nm, tk in last_pre:
                    w(v, nm, tk)
                inc("sDV", v.scalar_tensor_tensor(out=out_ap, in0=PS[:, pb:pb + 1024],
                                                  scalar=0.5, in1=ft_ap,
                                                  op0=ALU.mult, op1=ALU.add))
            return CNT["sDV"]

        def p1_xload(seq, tb):
            p = tb % 2
            gtb = seq * TBS + tb
            xslot = A1_X0 if p == 0 else A1_X1
            sX = "sXa" if p == 0 else "sXb"

            @on("sync")
            def _(sy):
                tkf = toks.get(("xfree", seq, tb - 2))
                if tkf is not None:
                    w(sy, "sDV", tkf)
                if seq == 0 and tb == 0:
                    w(sy, "sDV", TOK_CONSTS)
                if seq == 1 and tb == 0:
                    w(sy, "sDV", toks[("p3done", 0)])
                inc(sX, sy.dma_start(out=af(xslot, 1024),
                                     in_=xd[128 * gtb:128 * (gtb + 1), :]), 16)
            toks[("xload", seq, tb)] = CNT[sX]

        def p1_ln1_stats(seq, tb):
            p = tb % 2
            xslot = A1_X0 if p == 0 else A1_X1
            sX = "sXa" if p == 0 else "sXb"
            pre = [(sX, toks[("xload", seq, tb)])]
            if seq == 0 and tb == 0:
                pre.append(("sDV", TOK_CONSTS))
            toks[("ln1i", seq, tb)] = emit_ln_stats(af(xslot, 1024), 2, pre=pre)

        def p1_ft(seq, tb):
            p = tb % 2
            xslot = A1_X0 if p == 0 else A1_X1
            sX = "sXa" if p == 0 else "sXb"

            @on("vector")
            def _(v):
                w(v, sX, toks[("xload", seq, tb)])
                inc("sDV", v.tensor_add(af(FTs[p], 1024), halfb[:, :],
                                        af(xslot, 1024)))

        def p1_ln1_apply(seq, tb):
            p = tb % 2
            xslot = A1_X0 if p == 0 else A1_X1
            apply_pre = []
            tk = toks.get(("trT", seq, tb - 2))
            if tk is not None:
                apply_pre.append(("sPE", tk))
            sbi, t_rstd = toks[("ln1i", seq, tb)]
            t_h1 = emit_ln_apply(af(xslot, 1024), af(HLs[p], 1024), sbi, t_rstd,
                                 apply_pre=apply_pre)
            toks[("xfree", seq, tb)] = t_h1
            toks[("h1", seq, tb)] = t_h1

        def p1_iter(seq, k):
            if k < TBS:
                p = k % 2
                pe_pre = []
                if LAST.get("tr34_evac_dv") is not None:
                    pe_pre.append(LAST["tr34_evac_dv"])
                t_trT = emit_tr_pe(af(HLs[p], 1024), toks[("h1", seq, k)],
                                   pe_pre=pe_pre)
                toks[("trT", seq, k)] = t_trT

            if k >= 1:
                t_tsc = emit_ffn_gate(toks[("ffngl", seq, k - 1)],
                                      par=(k - 1) % 2)

            if k < TBS:
                p = k % 2
                evac_pre = []
                tk = toks.get(("ffngl", seq, k - 2))
                if tk is not None:
                    evac_pre.append(("sPE", tk))
                t_hT = emit_tr_evac(abf(HTs[p], 1024), t_trT, evac_pre=evac_pre,
                                    eng="scalar")
                LAST["tr34_evac_dv"] = t_hT
                toks[("hT", seq, k)] = t_hT

            if k >= 1:
                t_base = emit_ffn_lora(t_tsc, par=(k - 1) % 2)

            if k + 1 < TBS:
                p1_xload(seq, k + 1)
                p1_ln1_stats(seq, k + 1)

            if k >= 1:
                last_pre = []
                if LAST.get("outf_read") is not None:
                    last_pre.append(LAST["outf_read"])
                t_r1 = emit_ffn_last(af(A1_OUTF, 1024),
                                     af(FTs[(k - 1) % 2], 1024), t_base,
                                     par=(k - 1) % 2, last_pre=last_pre)
                toks[("ffnout", seq, k - 1)] = t_r1

                tb = k - 1
                gtb = seq * TBS + tb

                @on("sync")
                def _(sy):
                    w(sy, "sDV", toks[("ffnout", seq, tb)])
                    inc("sSP", sy.dma_start(
                        out=r1scr[128 * gtb:128 * (gtb + 1), :],
                        in_=af(A1_OUTF, 1024)), 16)
                LAST["outf_read"] = ("sSP", CNT["sSP"])

            if k + 1 < TBS:
                p1_ft(seq, k + 1)

            if k < TBS:
                p = k % 2
                ffn_pe_pre = []
                tk = toks.get(("ffnout", seq, k - 2))
                if tk is not None:
                    ffn_pe_pre.append(("sDV", tk))
                t_gl = emit_ffn_mm(abf(HTs[p], 1024), toks[("hT", seq, k)],
                                   par=p, pe_pre=ffn_pe_pre)
                toks[("ffngl", seq, k)] = t_gl

            if k >= 1:
                tb = k - 1
                toks[("ln2i", seq, tb)] = emit_ln_stats(af(A1_OUTF, 1024), 2)

            if k + 1 < TBS:
                p1_ln1_apply(seq, k + 1)

            if k >= 1:
                tb = k - 1
                p = tb % 2
                apply_pre2 = []
                tk = toks.get(("trH", seq, tb - 2))
                if tk is not None:
                    apply_pre2.append(("sPE", tk))
                sbi2, t_rstd2 = toks[("ln2i", seq, tb)]
                t_h2 = emit_ln_apply(af(A1_OUTF, 1024), af(H2Rs[p], 1024),
                                     sbi2, t_rstd2, apply_pre=apply_pre2,
                                     store_tb=tb)

                h2T_dst = bass.AP(h2T, 128 * tb,
                                  [[h2T.shape[1], 128], [2048, 8], [1, 128]])
                t_h2T, t_trh2 = emit_transpose_block(
                    af(H2Rs[p], 1024), h2T_dst, t_h2,
                    pe_pre=[LAST["tr34_evac_dv"]], eng="scalar")
                toks[("trH", seq, tb)] = t_trh2
                LAST["tr34_evac_dv"] = t_h2T
                toks[("h2T", seq, tb)] = t_h2T

        for seq in range(SEQ_PER_CORE):
            p1_xload(seq, 0)
            p1_ln1_stats(seq, 0)
            p1_ft(seq, 0)
            p1_ln1_apply(seq, 0)
            for k in range(TBS + 1):
                p1_iter(seq, k)
            toks[("p1done", seq)] = CNT["sDV"]

            if seq == 1:
                @on("gpsimd")
                def _(g, seq=seq):
                    w(g, "sPE", toks[("ty", 0, TBS - 1)])
                    inc("sWv", g.dma_start(
                        out=wvp.ap(),
                        in_=bass.AP(wvd.tensor, 0, [[1024, 128], [131072, 8], [1, 1024]])), 16)
                toks[("wvload", seq)] = CNT["sWv"]

            for tb in range(TBS):
                pair = tb % 3
                @on("tensor")
                def _(t, tb=tb, pair=pair, seq=seq):
                    tk2 = toks[("h2T", seq, tb)]
                    w(t, tk2[0], tk2[1])
                    w(t, "sWv", toks[("wvload", seq)])
                    tkf = toks.get(("vfree", seq, tb - 3))
                    if tkf is not None:
                        w(t, "sDV", tkf)
                    if tb == 0:
                        w(t, "sDV", toks[("p1done", seq)])
                        tk3 = toks[("h2T", seq, TBS - 1)]
                        w(t, tk3[0], tk3[1])
                    for c in range(8):
                        st = (c == 0)
                        t.matmul(PS[:, 1024 * pair:1024 * pair + 512],
                                 h2T.ap()[:, 2048 * c + 128 * tb:2048 * c + 128 * (tb + 1)],
                                 wvp.ap()[:, 1024 * c:1024 * c + 512],
                                 start=st, stop=(c == 7))
                        mm = t.matmul(PS[:, 1024 * pair + 512:1024 * (pair + 1)],
                                      h2T.ap()[:, 2048 * c + 128 * tb:2048 * c + 128 * (tb + 1)],
                                      wvp.ap()[:, 1024 * c + 512:1024 * (c + 1)],
                                      start=st, stop=(c == 7))
                    incd("sPE", "tensor", t, mm)
                t_vps = CNT["sPE"]

                @on("vector")
                def _(v, tb=tb, pair=pair, t_vps=t_vps):
                    w(v, "sPE", t_vps)
                    incd("sDV", "vector", v, v.tensor_copy(
                        bass.AP(vhat, 1040 * tb, [[vhat.shape[1], 128], [65, 16], [1, 64]]),
                        PS[:, 1024 * pair:1024 * (pair + 1)].rearrange(
                            "p (h d) -> p h d", d=64)))
                toks[("vfree", seq, tb)] = CNT["sDV"]
            toks[("vdone", seq)] = CNT["sDV"]

            @on("gpsimd")
            def _(g, seq=seq):
                w(g, "sDV", toks[("vdone", seq)])
                inc("sWv", g.dma_start(
                    out=wvp.ap(),
                    in_=bass.AP(wpd.tensor, 0, [[1024, 128], [131072, 8], [1, 1024]])), 16)
            toks[("wpload", seq)] = CNT["sWv"]

            for hp in range(HP):
                par = hp % 2
                wqo = A2_WQ0 if par == 0 else A2_WQ1
                wko = A2_WK0 if par == 0 else A2_WK1
                sWq = "sWq0" if par == 0 else "sWq1"

                @on("gpsimd")
                def _(g, hp=hp, wqo=wqo, wko=wko, seq=seq, sWq=sWq):
                    tkf = toks.get(("qkmmdone", seq, hp - 2))
                    if tkf is not None:
                        w(g, "sPE", tkf)
                    if hp < 2:
                        w(g, "sDV", toks[("p1done", seq)])
                    if seq == 1 and hp == 0:
                        w(g, "sPE", toks[("qkmmdone", 0, HP - 1)])
                    inc(sWq, g.dma_start(
                        out=abf(wqo, 1024),
                        in_=bass.AP(wqd.tensor, 128 * hp, [[1024, 128], [131072, 8], [1, 128]])), 16)
                    inc(sWq, g.dma_start(
                        out=abf(wko, 1024),
                        in_=bass.AP(wkd.tensor, 128 * hp, [[1024, 128], [131072, 8], [1, 128]])), 16)
                toks[("wqkload", seq, hp)] = CNT[sWq]

                @on("tensor")
                def _(t, hp=hp, wqo=wqo, wko=wko, seq=seq, sWq=sWq):
                    w(t, sWq, toks[("wqkload", seq, hp)])
                    if hp == 0:
                        w(t, "sDV", toks[("vdone", seq)])
                    else:
                        w(t, "sDV", toks[("qkevac", seq, hp - 1)])
                        w(t, "sAC", toks[("kevac", seq, hp - 1)])
                        w(t, "sDV", toks[("otevac", seq, hp - 1, QBS - 1)])
                    for c in range(8):
                        st = (c == 0)
                        for t5 in range(4):
                            t.matmul(PS[:, 512 * t5:512 * (t5 + 1)],
                                     abf(wqo, 1024)[:, 128 * c:128 * (c + 1)],
                                     h2T.ap()[:, 2048 * c + 512 * t5:2048 * c + 512 * (t5 + 1)],
                                     start=st, stop=(c == 7))
                        for t5 in range(4):
                            mm = t.matmul(PS[:, 2048 + 512 * t5:2048 + 512 * (t5 + 1)],
                                          abf(wko, 1024)[:, 128 * c:128 * (c + 1)],
                                          h2T.ap()[:, 2048 * c + 512 * t5:2048 * c + 512 * (t5 + 1)],
                                          start=st, stop=(c == 7))
                    incd("sPE", "tensor", t, mm)
                toks[("qkmmdone", seq, hp)] = CNT["sPE"]

                @on("vector")
                def _(v, hp=hp, seq=seq):
                    w(v, "sPE", toks[("qkmmdone", seq, hp)])
                    incd("sDV", "vector", v, v.tensor_copy(abf(A2_QT, 2048), PS[:, 0:2048]))
                toks[("qkevac", seq, hp)] = CNT["sDV"]

                @on("scalar")
                def _(s, hp=hp, seq=seq):
                    w(s, "sPE", toks[("qkmmdone", seq, hp)])
                    incd("sAC", "scalar", s, s.activation(
                        out=abf(A2_KT, 2048), in_=PS[:, 2048:4096],
                        func=AF.Identity))
                toks[("kevac", seq, hp)] = CNT["sAC"]

                for qb in range(QBS):
                    ntile = 2 * KBS
                    groups = [list(range(3 * i, min(3 * i + 3, ntile)))
                              for i in range((ntile + 2) // 3)]
                    exp_tok = {}
                    avg_tok = {}

                    def ptA(r):
                        return (abf(A2_PT0, 1536) if r == 0 else
                                abf(A2_PT1, 1536) if r == 1 else
                                hh3.ap()[:, 0:768].bitcast(BF16))

                    def emit_scores(gi, gr):
                        rpar = gi % 2

                        @on("tensor")
                        def _(t, hp=hp, qb=qb, gi=gi, gr=gr, seq=seq, rpar=rpar):
                            if gi == 0:
                                w(t, "sDV", toks[("qkevac", seq, hp)])
                                w(t, "sAC", toks[("kevac", seq, hp)])
                            if gi >= 2:
                                w(t, "sAC", exp_tok[gi - 2])
                            for j, ti in enumerate(gr):
                                head = ti % 2
                                kb = ti // 2
                                slot = 3 * rpar + j
                                mm = t.matmul(
                                    PS[:, 512 * slot:512 * (slot + 1)],
                                    abf(A2_KT, 2048, 64 * head, 64 * head + 64)[:, 128 * kb:128 * (kb + 1)],
                                    abf(A2_QT, 2048, 64 * head, 64 * head + 64)[:, 512 * qb:512 * (qb + 1)],
                                    start=True, stop=True)
                            incd("sPE", "tensor", t, mm)
                        toks[("sc", seq, hp, qb, gi)] = CNT["sPE"]

                        @on_raw("scalar")
                        def _(s, hp=hp, qb=qb, gi=gi, gr=gr, seq=seq, rpar=rpar):
                            w(s, "sPE", toks[("sc", seq, hp, qb, gi)])
                            if gi >= 3:
                                w(s, "sPE", avg_tok[gi - 3])
                            ncols = 512 * len(gr)
                            incd("sAC", "scalar", s, s.activation(
                                out=ptA(gi % 3)[:, 0:ncols],
                                in_=PS[:, 1536 * rpar:1536 * rpar + ncols],
                                func=AF.Exp,
                                scale=cpool[:, 0:1], bias=cpool[:, 1:2]))
                        exp_tok[gi] = CNT["sAC"]

                    def emit_av(gi, gr):
                        rpar = gi % 2

                        @on("tensor")
                        def _(t, hp=hp, qb=qb, gi=gi, gr=gr, seq=seq, rpar=rpar):
                            w(t, "sAC", exp_tok[gi])
                            if gi == 0:
                                if qb == 0:
                                    tk = toks.get(("otevac", seq, hp - 1, QBS - 1))
                                else:
                                    tk = toks.get(("otevac", seq, hp, qb - 1))
                                if tk is not None:
                                    w(t, "sDV", tk)
                            pto = ptA(gi % 3)
                            for j, ti in enumerate(gr):
                                head = ti % 2
                                kb = ti // 2
                                mm = t.matmul(
                                    PS[0:65, 3072 + 512 * head:3072 + 512 * head + 512],
                                    vhat.ap()[:, 1040 * kb + 65 * (2 * hp + head):
                                              1040 * kb + 65 * (2 * hp + head) + 65],
                                    pto[:, 512 * j:512 * (j + 1)],
                                    start=(kb == 0), stop=(kb == KBS - 1))
                            incd("sPE", "tensor", t, mm)
                        avg_tok[gi] = CNT["sPE"]

                    for gi, gr in enumerate(groups):
                        emit_scores(gi, gr)
                        if gi >= 1:
                            emit_av(gi - 1, groups[gi - 1])
                    emit_av(len(groups) - 1, groups[-1])
                    toks[("av", seq, hp, qb)] = avg_tok[len(groups) - 1]

                    @on_raw("vector")
                    def _(v, hp=hp, qb=qb, seq=seq):
                        w(v, "sPE", toks[("av", seq, hp, qb)])
                        if LAST.get("bstg_free") is not None:
                            w(v, "sBm", LAST["bstg_free"])
                        if LAST.get("scr_free") is not None:
                            w(v, "sGP", LAST["scr_free"])
                        if LAST.get("reca_free") is not None:
                            w(v, "sGP", LAST["reca_free"])
                        v.tensor_copy(
                            arena.ap()[0:64, A2_OTF + 512 * qb:A2_OTF + 512 * (qb + 1)],
                            PS[0:64, 3072:3584])
                        v.tensor_copy(bstg[:, :], PS[0:64, 3584:4096])
                        v.tensor_copy(
                            arena.ap()[0:1, A2_RECB + 512 * qb:A2_RECB + 512 * (qb + 1)],
                            PS[64:65, 3072:3584])
                        incd("sDV", "vector", v, v.tensor_copy(arena.ap()[0:1, A2_SCR:A2_SCR + 512],
                                                 PS[64:65, 3584:4096]))
                    toks[("otevac", seq, hp, qb)] = CNT["sDV"]

                    @on("sync")
                    def _(sy, hp=hp, qb=qb, seq=seq):
                        w(sy, "sDV", toks[("otevac", seq, hp, qb)])
                        inc("sBm", sy.dma_start(
                            out=arena.ap()[64:128, A2_OTF + 512 * qb:A2_OTF + 512 * (qb + 1)],
                            in_=bstg[:, :]), 16)
                    LAST["bstg_free"] = CNT["sBm"]
                    toks[("bdma", seq, hp, qb)] = CNT["sBm"]

                    @on("gpsimd")
                    def _(g, hp=hp, qb=qb, seq=seq):
                        w(g, "sDV", toks[("otevac", seq, hp, qb)])
                        if hp == 0:
                            w(g, "sSP")
                            w(g, "sOU")
                        if LAST.get("reca_free") is not None:
                            w(g, "sGP", LAST["reca_free"])
                        inc("sGP", g.partition_broadcast(
                            arena.ap()[0:128, A2_SUMB + 512 * qb:A2_SUMB + 512 * (qb + 1)],
                            arena.ap()[0:1, A2_SCR:A2_SCR + 512], channels=128))
                    LAST["scr_free"] = CNT["sGP"]
                    toks[("bsumb", seq, hp, qb)] = CNT["sGP"]

                @on("gpsimd")
                def _(g, hp=hp, seq=seq):
                    w(g, "sDV", toks[("otevac", seq, hp, QBS - 1)])
                    w(g, "sGP", toks[("bsumb", seq, hp, QBS - 1)])
                    inc("sGP", g.partition_broadcast(af(A2_SUMB, 2048, 0, 64),
                                          arena.ap()[0:1, A2_RECB:A2_RECB + 2048],
                                          channels=64))
                toks[("bsuma", seq, hp)] = CNT["sGP"]
                LAST["reca_free"] = CNT["sGP"]

                @on("vector")
                def _(v, hp=hp, seq=seq):
                    w(v, "sGP", toks[("bsuma", seq, hp)])
                    w(v, "sBm", toks[("bdma", seq, hp, QBS - 1)])
                    inc("sDV", v.reciprocal(af(A2_RECB, 2048), af(A2_SUMB, 2048)))
                    w(v, "sDV")
                    inc("sDV", v.tensor_tensor(
                        out=oT.ap()[:, 2048 * hp:2048 * (hp + 1)],
                        in0=af(A2_OTF, 2048), in1=af(A2_RECB, 2048),
                        op=ALU.mult))
                toks[("mixdone", seq, hp)] = CNT["sDV"]
            toks[("p2done", seq)] = CNT["sDV"]

            H2R3s = [A1_H2R, A1_X0]
            FT3s = [A1_FT, A1_X1]
            HT3s = [A1_HT, 11776]
            LAST["hl_read_pe"] = toks[("trT", seq, TBS - 2)]
            toks[("ffngl3", seq, -2)] = toks[("ffngl", seq, TBS - 2)]
            toks[("ffnout3", seq, -1)] = toks[("ffnout", seq, TBS - 2)]

            def p3_r1load(k):
                gtb = seq * TBS + k
                rslot = A1_R10 if k % 2 == 0 else A1_R11
                sR = "sRa" if k % 2 == 0 else "sRb"

                @on("sync")
                def _(sy):
                    tkf = toks.get(("r1free", seq, k - 2))
                    if tkf is not None:
                        w(sy, "sDV", tkf)
                    if k == 0:
                        w(sy, "sDV", toks[("p2done", seq)])
                        w(sy, "sSP")
                    inc(sR, sy.dma_start(out=af(rslot, 1024),
                                         in_=r1scr[128 * gtb:128 * (gtb + 1), :]), 16)
                toks[("r1load", seq, k)] = CNT[sR]

            def p3_pre(k):
                p = k % 2
                rslot = A1_R10 if p == 0 else A1_R11
                sR = "sRa" if p == 0 else "sRb"

                @on("vector")
                def _(v):
                    w(v, sR, toks[("r1load", seq, k)])
                    inc("sDV", v.tensor_scalar(
                        out=af(H2R3s[p], 1024), in0=af(rslot, 1024),
                        scalar1=ln2m[:, k:k + 1], scalar2=ln2r[:, k:k + 1],
                        op0=ALU.subtract, op1=ALU.mult))
                    w(v, "sDV")
                    v.tensor_add(af(H2R3s[p], 1024), af(H2R3s[p], 1024),
                                 bprojb[:, :])
                    inc("sDV", v.tensor_add(af(FT3s[p], 1024), betab[:, :],
                                            af(rslot, 1024)))
                toks[("r1free", seq, k)] = CNT["sDV"]

            def p3_proj(k):
                @on("tensor")
                def _(t):
                    w(t, "sDV", toks[("mixdone", seq, HP - 1)])
                    w(t, "sWv", toks[("wpload", seq)])
                    tky = toks.get(("pyread", seq, k - 1))
                    if tky is not None:
                        w(t, "sDV", tky)
                    for hp in range(HP):
                        st = (hp == 0)
                        t.matmul(PS[:, 3072:3584],
                                 oT.ap()[:, 2048 * hp + 128 * k:2048 * hp + 128 * (k + 1)],
                                 wvp.ap()[:, 1024 * hp:1024 * hp + 512],
                                 start=st, stop=(hp == 7))
                        mm = t.matmul(PS[:, 3584:4096],
                                      oT.ap()[:, 2048 * hp + 128 * k:2048 * hp + 128 * (k + 1)],
                                      wvp.ap()[:, 1024 * hp + 512:1024 * (hp + 1)],
                                      start=st, stop=(hp == 7))
                    incd("sPE", "tensor", t, mm)
                toks[("ty", seq, k)] = CNT["sPE"]

            def p3_yread(k):
                p = k % 2
                sbi4 = LNROT[0] % 4
                LNROT[0] += 1
                bnst4, mv24 = bnstI(sbi4), mv2I(sbi4)
                rstd4, nb4 = rstdI(sbi4), nbI(sbi4)

                @on("vector")
                def _(v):
                    w(v, "sPE", toks[("ty", seq, k)])
                    tkx = toks.get(("pxh", seq, k - 1))
                    if tkx is not None:
                        w(v, "sAC", tkx)
                    inc("sDV", v.scalar_tensor_tensor(
                        out=af(A1_Y, 1024), in0=PS[:, 3072:4096],
                        scalar=1.0, in1=af(H2R3s[p], 1024),
                        op0=ALU.mult, op1=ALU.add))
                    toks[("pyread", seq, k)] = CNT["sDV"]
                    w(v, "sDV")
                    v.bn_stats(bnst4[:, 0:6], af(A1_Y, 512))
                    inc("sDV", v.bn_stats(bnst4[:, 6:12], af(A1_Y + 512, 512)))
                    w(v, "sDV")
                    inc("sDV", v.bn_aggr(mv24[:, :], bnst4[:, 0:12]))
                t_st4 = CNT["sDV"]

                @on("scalar")
                def _(s):
                    w(s, "sDV", t_st4)
                    inc("sAC", s.activation(out=rstd4[:, :], in_=mv24[:, 1:2],
                                            func=AF.Ln, bias=cpool[:, 3:4]))
                    w(s, "sAC")
                    inc("sAC", s.activation(out=rstd4[:, :], in_=rstd4[:, :],
                                            func=AF.Exp, scale=cpool[:, 4:5]))
                t_r4 = CNT["sAC"]

                @on("vector")
                def _(v):
                    w(v, "sAC", t_r4)
                    inc("sDV", v.scalar_tensor_tensor(
                        out=nb4[:, :], in0=mv24[:, 0:1], scalar=-1.0,
                        in1=rstd4[:, :], op0=ALU.mult, op1=ALU.mult))
                t_nb = CNT["sDV"]

                @on("scalar")
                def _(s):
                    w(s, "sDV", t_nb)
                    inc("sAC", s.activation(out=af(A1_XH, 1024), in_=af(A1_Y, 1024),
                                            func=AF.Identity,
                                            scale=rstd4[:, :], bias=nb4[:, :]))
                toks[("pxh", seq, k)] = CNT["sAC"]

            def p3_tail(k):
                p = k % 2

                @on("vector")
                def _(v):
                    w(v, "sAC", toks[("pxh", seq, k)])
                    inc("sDV", v.tensor_mul(af(A1_XH, 1024), af(A1_XH, 1024),
                                            gammab[:, :]))
                    w(v, "sDV")
                    inc("sDV", v.tensor_add(af(A1_R2, 1024), af(A1_XH, 1024),
                                            af(FT3s[p], 1024)))
                t_r2 = CNT["sDV"]

                t_h3 = emit_ln(af(A1_R2, 1024), af(A1_HL, 1024), 2,
                               pre=[("sDV", t_r2)],
                               gap=lambda v: v.tensor_add(
                                   hh3.ap()[:, 1024 * p:1024 * (p + 1)],
                                   halfb[:, :], af(A1_R2, 1024)),
                               apply_pre=[("sPE", LAST["hl_read_pe"])])

                t_tr3 = emit_tr_pe(af(A1_HL, 1024), t_h3,
                                   pe_pre=[LAST["tr34_evac_dv"]])
                LAST["hl_read_pe"] = t_tr3

                evac_pre = []
                tk = toks.get(("ffngl3", seq, k - 2))
                if tk is not None:
                    evac_pre.append(("sPE", tk))
                t_hT3 = emit_tr_evac(abf(HT3s[p], 1024), t_tr3, evac_pre=evac_pre,
                                     eng="scalar")
                LAST["tr34_evac_dv"] = t_hT3
                toks[("hT3", seq, k)] = t_hT3

            def p3_ffn(k):
                p = k % 2
                pe_pre = []
                tk = toks.get(("ffnout3", seq, k - 1))
                if tk is not None:
                    pe_pre.append(("sDV", tk))
                t_gl = emit_ffn_mm(abf(HT3s[p], 1024), toks[("hT3", seq, k)],
                                   par=0, pe_pre=pe_pre)
                toks[("ffngl3", seq, k)] = t_gl

            def p3_out(k):
                p = k % 2
                gtb = seq * TBS + k
                t_tsc2 = emit_ffn_gate(toks[("ffngl3", seq, k)], par=0)
                t_out = emit_ffn_tail2(af(A1_OUTF, 1024),
                                       hh3.ap()[:, 1024 * p:1024 * (p + 1)],
                                       t_tsc2, par=0,
                                       last_pre=[LAST["outf_read"]])
                toks[("ffnout3", seq, k)] = t_out

                @on("sync")
                def _(sy):
                    w(sy, "sDV", t_out)
                    inc("sOU", sy.dma_start(out=outd[128 * gtb:128 * (gtb + 1), :],
                                            in_=af(A1_OUTF, 1024)), 16)
                LAST["outf_read"] = ("sOU", CNT["sOU"])

            p3_r1load(0)
            p3_pre(0)
            p3_proj(0)
            for k in range(TBS + 2):
                if k < TBS:
                    p3_yread(k)
                if k + 1 < TBS:
                    p3_r1load(k + 1)
                if 2 <= k:
                    p3_out(k - 2)
                if 1 <= k <= TBS:
                    p3_ffn(k - 1)
                if k + 1 < TBS:
                    p3_pre(k + 1)
                    p3_proj(k + 1)
                if k < TBS:
                    p3_tail(k)
            toks[("p3done", seq)] = CNT["sDV"]

        @on("sync")
        def _(sy):
            w(sy, "sOU")
            w(sy, "sSP")
            w(sy, "sBm")

    es.close()
    mybir.codegen_inst_isa_subclasses(nc)
    return nc


_NC_CACHE = None


def kernel(L0_attn_x, L0_attn_w_q, L0_attn_w_k, L0_attn_w_v, L0_attn_w_proj,
           L0_attn_b_proj, L0_attn_ln_gamma, L0_attn_ln_beta,
           L0_ffn_W_lin, L0_ffn_b_lin, L0_ffn_gate_w, L0_ffn_left, L0_ffn_right):
    global _NC_CACHE
    if _NC_CACHE is None:
        _NC_CACHE = build_nc()
    nc = _NC_CACHE

    f = np.float32
    x = np.ascontiguousarray(np.asarray(L0_attn_x, dtype=f))
    wq = np.ascontiguousarray(np.asarray(L0_attn_w_q, dtype=f))
    wk = np.ascontiguousarray(np.asarray(L0_attn_w_k, dtype=f))
    wv = np.ascontiguousarray(np.asarray(L0_attn_w_v, dtype=f))
    wp = np.ascontiguousarray(np.asarray(L0_attn_w_proj, dtype=f))
    wlt = np.ascontiguousarray(np.asarray(L0_ffn_W_lin, dtype=f).T)
    gl = np.ascontiguousarray(np.concatenate(
        [np.asarray(L0_ffn_gate_w, dtype=f).T,
         np.asarray(L0_ffn_left, dtype=f).transpose(1, 0, 2).reshape(D, 32)], axis=1))
    rgtv = np.ascontiguousarray(np.asarray(L0_ffn_right, dtype=f).reshape(32, D))
    vecs = np.ascontiguousarray(np.stack(
        [np.asarray(L0_attn_b_proj, dtype=f),
         np.asarray(L0_ffn_b_lin, dtype=f),
         np.asarray(L0_attn_ln_gamma, dtype=f),
         np.asarray(L0_attn_ln_beta, dtype=f)], axis=0))

    in_maps = []
    for c in range(NCORES):
        xs = np.ascontiguousarray(
            x[SEQ_PER_CORE * c:SEQ_PER_CORE * (c + 1)].reshape(TOK, D))
        in_maps.append({"x": xs, "wq": wq, "wk": wk, "wv": wv, "wp": wp,
                        "wlt": wlt, "gld": gl, "rgt": rgtv, "vec": vecs})

    res = run_bass_kernel_spmd(nc, in_maps, core_ids=list(range(NCORES)))
    out = np.concatenate(
        [r["out"].reshape(SEQ_PER_CORE, S, D) for r in res.results], axis=0)
    return out.astype(np.float32)
